# version 13
# speedup vs baseline: 1.0977x; 1.0050x over previous
import numpy as np
from contextlib import ExitStack

import concourse.bass as bass
import concourse.mybir as mybir
from concourse.bass_utils import run_bass_kernel_spmd

F32 = mybir.dt.float32
BF16 = mybir.dt.bfloat16
I16 = mybir.dt.int16
AF = mybir.ActivationFunctionType
ALU = mybir.AluOpType
AX = mybir.AxisListType

H = 16
HKV = 4
G = H // HKV
DH = 128
DM = 2048
FF = 8192
FD = 768
NFC = FD // 128
NDC = DM // 128
EPS = 1e-8
LN_EPS = 1e-5
L2_EPS = 1e-12
Q_EPS = 1e-4
ATANH_HALF = 0.5493061443340549
NU = 3
NCORES = 8


class Sem:

    def __init__(self, handle):
        self.h = handle
        self.v = 0

    def inc(self, n):
        self.v += n
        return self.v


class Ctx:
    pass



def dma_bcast_row(c, eng, dst, hbm_vec_ap, sem):
    src = bass.AP(hbm_vec_ap.tensor, hbm_vec_ap.offset,
                  [[0, 128]] + list(hbm_vec_ap.ap))
    return dma(c, eng, sem, dst, src)


def hilo_split(c, src_f32, dst_hi, dst_lo, cast_sem):
    nc = c.nc
    hi_done = dma(c, nc.gpsimd, cast_sem, dst_hi, src_f32)
    nc.vector.wait_ge(cast_sem.h, hi_done)
    nc.vector.tensor_sub(dst_lo, src_f32, dst_hi).then_inc(c.s_dve.h, 1)
    return hi_done, c.s_dve.inc(1)


def dma(c, eng, sem, out, in_, **kw):
    eng.wait_ge(sem.h, sem.v)
    eng.dma_start(out=out, in_=in_, **kw).then_inc(sem.h, 16)
    return sem.inc(16)


def dma_group(c, eng, sem, pairs, **kw):
    eng.wait_ge(sem.h, sem.v)
    for out, in_ in pairs:
        eng.dma_start(out=out, in_=in_, **kw).then_inc(sem.h, 16)
    return sem.inc(16 * len(pairs))


def vf(c, inst):
    inst.then_inc(c.s_dve.h, 1)
    c.nc.vector.wait_ge(c.s_dve.h, c.s_dve.inc(1))


def va(c, inst):
    inst.then_inc(c.s_wm.h, 1)
    c.nc.vector.wait_ge(c.s_wm.h, c.s_wm.inc(1))


def af(c, inst):
    inst.then_inc(c.s_act.h, 1)
    c.nc.scalar.wait_ge(c.s_act.h, c.s_act.inc(1))


def gf(c, inst):
    inst.then_inc(c.s_aux.h, 1)
    c.nc.gpsimd.wait_ge(c.s_aux.h, c.s_aux.inc(1))


def load_split_pipelined(c, WT, col0, ncols, wh, wl, stage2, ld_sem, ca_sem,
                         chunk_dc=4):
    nc = c.nc
    nrow = WT.shape[0] // 128
    nc.sync.wait_ge(ca_sem.h, ca_sem.v)
    nc.sync.wait_ge(c.s_dve.h, c.s_dve.v)
    slot_cast = [None, None]
    slot_sub = [None, None]
    for qq in range(nrow // chunk_dc):
        s = qq % 2
        if slot_cast[s] is not None:
            nc.sync.wait_ge(c.d["ca0"].h, slot_cast[s])
        if slot_sub[s] is not None:
            nc.sync.wait_ge(c.s_dve.h, slot_sub[s])
        w_ld = dma_group(c, nc.sync, ld_sem,
                         [(stage2[:, s, i * ncols:(i + 1) * ncols],
                           WT[(qq * chunk_dc + i) * 128:(qq * chunk_dc + i + 1) * 128,
                              col0:col0 + ncols])
                          for i in range(chunk_dc)])
        nc.gpsimd.wait_ge(ld_sem.h, w_ld)
        o0 = qq * chunk_dc * ncols
        o1 = (qq + 1) * chunk_dc * ncols
        slot_cast[s] = dma(c, nc.gpsimd, ca_sem, wh[:, o0:o1], stage2[:, s])
        nc.vector.wait_ge(ca_sem.h, slot_cast[s])
        nc.vector.tensor_sub(wl[:, o0:o1], stage2[:, s],
                             wh[:, o0:o1]).then_inc(c.s_dve.h, 1)
        slot_sub[s] = c.s_dve.inc(1)
    return c.s_dve.v


def transpose_tiles(c, src_bf16, nch, stage_dst, ps_tr, tr_free):
    nc = c.nc
    done = None
    for q0 in range(0, nch, 8):
        nb = min(8, nch - q0)
        bank = (q0 // 8) % 2
        if tr_free[bank] is not None:
            nc.tensor.wait_ge(c.s_act.h, tr_free[bank])
        inst = None
        for j in range(nb):
            dc = q0 + j
            inst = nc.tensor.matmul(
                ps_tr[:, bank, j * 128:(j + 1) * 128],
                src_bf16[:, dc * 128:(dc + 1) * 128], c.ident[:],
                is_transpose=True, start=(j == 0), stop=(j == nb - 1))
        inst.then_inc(c.s_pe.h, 1)
        tr_done = c.s_pe.inc(1)
        nc.scalar.wait_ge(c.s_pe.h, tr_done)
        nc.scalar.copy(stage_dst[:, q0 * 128:(q0 + nb) * 128],
                       ps_tr[:, bank, 0:nb * 128]).then_inc(c.s_act.h, 1)
        done = c.s_act.inc(1)
        tr_free[bank] = done
    return done



def build_program(ntc, debug=(), phases=("wmean", "p1", "mb", "v", "kside",
                                          "delta", "cc", "ffn", "qside")):
    nc = bass.Bass("TRN2", target_bir_lowering=False, num_devices=NCORES)
    T = ntc * 128
    c = Ctx()
    c.nc = nc
    c.ntc = ntc
    c.T = T

    def scratch(name, shape, dtype):
        kind = "ExternalOutput" if name in debug else "Internal"
        return nc.dram_tensor(name, shape, dtype, kind=kind).ap()

    c.hs = nc.dram_tensor("hs", [T, DM], F32, kind="ExternalInput").ap()
    c.WqT = nc.dram_tensor("WqT", [DM, DM], F32, kind="ExternalInput").ap()
    c.WkT = nc.dram_tensor("WkT", [DM, HKV * DH], F32, kind="ExternalInput").ap()
    c.WvT = nc.dram_tensor("WvT", [DM, HKV * DH], F32, kind="ExternalInput").ap()
    c.WgT = nc.dram_tensor("WgT", [DM, DM], F32, kind="ExternalInput").ap()
    c.WmbT = nc.dram_tensor("WmbT", [DM, DM], F32, kind="ExternalInput").ap()
    c.bq = nc.dram_tensor("bq", [1, DM], F32, kind="ExternalInput").ap()
    c.bk = nc.dram_tensor("bk", [1, HKV * DH], F32, kind="ExternalInput").ap()
    c.bv = nc.dram_tensor("bv", [1, HKV * DH], F32, kind="ExternalInput").ap()
    c.bg = nc.dram_tensor("bg", [1, DM], F32, kind="ExternalInput").ap()
    c.bmb = nc.dram_tensor("bmb", [1, DM], F32, kind="ExternalInput").ap()
    c.mem = nc.dram_tensor("mem", [H, FD, DH], F32, kind="ExternalInput").ap()
    c.nrm = nc.dram_tensor("nrm", [H, FD], F32, kind="ExternalInput").ap()
    c.w1T = nc.dram_tensor("w1T", [DM, FF], F32, kind="ExternalInput").ap()
    c.w2T = nc.dram_tensor("w2T", [DM, FF], F32, kind="ExternalInput").ap()
    c.w3T = nc.dram_tensor("w3T", [FF, DM], F32, kind="ExternalInput").ap()
    c.g1 = nc.dram_tensor("g1", [1, DM], F32, kind="ExternalInput").ap()
    c.g2 = nc.dram_tensor("g2", [1, DM], F32, kind="ExternalInput").ap()
    c.g3 = nc.dram_tensor("g3", [1, FF], F32, kind="ExternalInput").ap()

    c.out = nc.dram_tensor("out", [T, DM], F32, kind="ExternalOutput").ap()

    c.xnfm_h = scratch("xnfm_h", [ntc, 128, DM], BF16)
    c.xnfm_l = scratch("xnfm_l", [ntc, 128, DM], BF16)
    c.k1fm = scratch("k1fm", [ntc, 128, DM], BF16)
    c.k2fm = scratch("k2fm", [ntc, 128, DM], BF16)
    c.k3fm = scratch("k3fm", [ntc, 128, FF], BF16)
    c.mb_s = scratch("mb_s", [ntc, 128, DM], F32)
    c.v_s = scratch("v_s", [ntc, 128, HKV * DH], F32)
    c.ukh_s = scratch("ukh_s", [ntc, 128, HKV * FD], BF16)
    c.ukl_s = scratch("ukl_s", [ntc, 128, HKV * FD], BF16)
    c.wmvh_s = scratch("wmvh_s", [ntc, 128, DM], BF16)
    c.wmvl_s = scratch("wmvl_s", [ntc, 128, DM], BF16)
    c.h_s = scratch("h_s", [ntc, 128, FF], F32)
    c.ffn_s = scratch("ffn_s", [ntc, 128, DM], F32)
    c.cc_m = scratch("cc_m", [G, NFC, 128, G * DH], F32)
    c.cc_n = scratch("cc_n", [G, 128, NFC * G], F32)
    c.rowdbg = scratch("rowdbg", [128, 64], F32)
    c.xn_dbg = scratch("xn_dbg", [ntc, 128, DM], F32)
    c.debug_names = set(debug)

    es = ExitStack()
    c.es = es

    c.ident = es.enter_context(nc.sbuf_tensor("ident", [128, 128], BF16))
    c.onecol = es.enter_context(nc.sbuf_tensor("onecol", [128, 1], F32))
    c.halfcol = es.enter_context(nc.sbuf_tensor("halfcol", [128, 1], F32))
    c.r_mean = es.enter_context(nc.sbuf_tensor("r_mean", [128, ntc], F32))
    c.r_nrstd = es.enter_context(nc.sbuf_tensor("r_nrstd", [128, ntc], F32))
    c.r_rstd = es.enter_context(nc.sbuf_tensor("r_rstd", [128, ntc], F32))
    c.r_sc1 = es.enter_context(nc.sbuf_tensor("r_sc1", [128, ntc], F32))
    c.r_sc2 = es.enter_context(nc.sbuf_tensor("r_sc2", [128, ntc], F32))
    c.r_sc3 = es.enter_context(nc.sbuf_tensor("r_sc3", [128, ntc], F32))
    c.r_coef = es.enter_context(nc.sbuf_tensor("r_coef", [128, ntc * H], F32))
    c.thr1 = es.enter_context(nc.sbuf_tensor("thr1", [128, 1], F32))
    c.thr2 = es.enter_context(nc.sbuf_tensor("thr2", [128, 1], F32))
    c.thr3 = es.enter_context(nc.sbuf_tensor("thr3", [128, 1], F32))
    c.nthr1 = es.enter_context(nc.sbuf_tensor("nthr1", [128, 1], F32))
    c.nthr2 = es.enter_context(nc.sbuf_tensor("nthr2", [128, 1], F32))
    c.nthr3 = es.enter_context(nc.sbuf_tensor("nthr3", [128, 1], F32))
    c.acol1 = es.enter_context(nc.sbuf_tensor("acol1", [128, 1], F32))
    c.acol2 = es.enter_context(nc.sbuf_tensor("acol2", [128, 1], F32))
    c.acol3 = es.enter_context(nc.sbuf_tensor("acol3", [128, 1], F32))

    for name in ["pe", "dve", "act", "ccs", "aux", "wm", "pl"]:
        setattr(c, "s_" + name,
                Sem(es.enter_context(nc.semaphore("sem_" + name))))
    c.d = {}
    for name in ["ld0", "ld1", "ld2", "ld3", "st0", "st1", "st2", "st3",
                 "ca0", "ca1", "ca2", "ca3", "wt0", "wt1", "bc0", "dbg"]:
        c.d[name] = Sem(es.enter_context(nc.semaphore("dma_" + name)))

    c.wmean_es = None
    c.block = es.enter_context(nc.Block())

    phase_consts(c)
    nc.tensor.wait_ge(c.s_aux.h, c.ident_ready)
    c.wmean_pieces = []
    c.wmean_finish = None
    if "wmean" in phases:
        phase_wmean(c)
    if "p1" in phases:
        phase_p1(c)
    if c.wmean_finish is not None:
        c.wmean_finish()
    nc.all_engine_barrier()
    if c.wmean_es is not None:
        c.wmean_es.close()
    if "mb" in phases:
        phase_proj_spill(c, c.WmbT, c.bmb, DM, c.mb_s, sigmoid=True)
        nc.all_engine_barrier()
    if "v" in phases:
        phase_proj_spill(c, c.WvT, c.bv, HKV * DH, c.v_s, sigmoid=False)
        nc.all_engine_barrier()
    if "kside" in phases:
        phase_kside(c)
        nc.all_engine_barrier()
    c.use_cc = "cc" in phases
    if "delta" in phases:
        phase_delta(c)
        nc.all_engine_barrier()
    if "ffn" in phases:
        phase_ffn(c)
        nc.all_engine_barrier()
    if "qside" in phases:
        phase_qside(c)
    for name in ("st0", "st1", "st2", "st3", "dbg"):
        nc.sync.wait_ge(c.d[name].h, c.d[name].v)
    for name in ("ca0", "ca1", "ca2", "ca3", "bc0"):
        nc.gpsimd.wait_ge(c.d[name].h, c.d[name].v)

    es.close()
    return nc



def phase_consts(c):
    nc = c.nc
    nc.vector.memset(c.ident[:], 0.0).then_inc(c.s_dve.h, 1)
    nc.gpsimd.wait_ge(c.s_dve.h, c.s_dve.inc(1))
    nc.gpsimd.affine_select(
        out=c.ident[:], in_=c.ident[:], compare_op=ALU.not_equal,
        fill=1.0, base=0, pattern=[[-1, 128]], channel_multiplier=1,
    ).then_inc(c.s_aux.h, 1)
    c.ident_ready = c.s_aux.inc(1)
    nc.vector.memset(c.onecol[:], 1.0)
    nc.vector.memset(c.halfcol[:], 0.5).then_inc(c.s_dve.h, 1)
    c.consts_ready = c.s_dve.inc(1)



def phase_wmean(c):
    nc = c.nc
    es2 = ExitStack()
    c.wmean_es = es2
    CH = 4096
    buf = es2.enter_context(nc.sbuf_tensor("wm_buf", [128, 2, CH], F32))
    acc = es2.enter_context(nc.sbuf_tensor("wm_acc", [128, 3], F32))
    red = es2.enter_context(nc.sbuf_tensor("wm_red", [128, 1], F32))
    srow = es2.enter_context(nc.sbuf_tensor("wm_srow", [1, 8], F32))
    arow = es2.enter_context(nc.sbuf_tensor("wm_arow", [1, 3], F32))
    ps = es2.enter_context(nc.psum_tensor("wm_ps", [1, 8], F32))

    nc.vector.memset(acc[:], 0.0).then_inc(c.s_wm.h, 1)
    nc.vector.wait_ge(c.s_wm.h, c.s_wm.inc(1))

    mats = [(c.w1T, DM * FF, 0), (c.w2T, DM * FF, 1), (c.w3T, FF * DM, 2)]
    ld_sems = [c.d["ld2"], c.d["ld3"]]
    slot_free = [None, None]
    units = []
    for mi, (w, nelem, col) in enumerate(mats):
        rows_, cols = w.shape
        mch = min(CH, cols)
        for r in range(rows_ // 128):
            for cc0 in range(0, cols, mch):
                units.append((w, r, cc0, mch, col))
    state = {"si": 0}

    def wmean_piece(i):
        w, r, cc0, mch, col = units[i]
        si = state["si"]
        if slot_free[si] is not None:
            nc.sync.wait_ge(c.s_wm.h, slot_free[si])
        v = dma(c, nc.sync, ld_sems[si], buf[:, si, 0:mch],
                w[r * 128:(r + 1) * 128, cc0:cc0 + mch])
        nc.vector.wait_ge(ld_sems[si].h, v)
        va(c, nc.vector.tensor_reduce(
            red[:], buf[:, si, 0:mch], AX.X, ALU.add,
            apply_absolute_value=True))
        va(c, nc.vector.tensor_add(acc[:, col:col + 1],
                                   acc[:, col:col + 1], red[:]))
        slot_free[si] = c.s_wm.v
        state["si"] ^= 1

    c.wmean_pieces = [lambda i=i: wmean_piece(i) for i in range(len(units))]
    def wmean_finish():
        nc.tensor.wait_ge(c.s_wm.h, c.s_wm.v)
        nc.tensor.wait_ge(c.s_dve.h, c.consts_ready)
        nc.tensor.matmul(ps[0:1, 0:3], c.onecol[:], acc[:, 0:3],
                         start=True, stop=True).then_inc(c.s_pe.h, 1)
        pe_done = c.s_pe.inc(1)
        nc.vector.wait_ge(c.s_pe.h, pe_done)
        for i, (_, nelem, _c) in enumerate(mats):
            va(c, nc.vector.tensor_scalar_mul(srow[0:1, i:i + 1], ps[0:1, i:i + 1],
                                              1.0 / nelem))
        va(c, nc.vector.tensor_scalar_add(srow[0:1, 3:6], srow[0:1, 0:3], EPS))
        nc.vector.tensor_scalar_mul(srow[0:1, 3:6], srow[0:1, 3:6],
                                    ATANH_HALF).then_inc(c.s_wm.h, 1)
        rows_done = c.s_wm.inc(1)

        def atanh_col(i, dst):
            nc.scalar.wait_ge(c.s_wm.h, rows_done)
            nc.scalar.wait_ge(c.s_wm.h, c.s_wm.v)
            af(c, nc.scalar.activation(srow[0:1, 6:7], srow[0:1, i:i + 1], AF.Ln,
                                       bias=c.onecol[0:1, :], scale=1.0))
            nc.scalar.activation(srow[0:1, 7:8], srow[0:1, i:i + 1], AF.Ln,
                                 bias=c.onecol[0:1, :], scale=-1.0).then_inc(c.s_act.h, 1)
            d = c.s_act.inc(1)
            nc.vector.wait_ge(c.s_act.h, d)
            va(c, nc.vector.tensor_sub(dst, srow[0:1, 6:7], srow[0:1, 7:8]))
            nc.vector.tensor_scalar_mul(dst, dst, 0.5).then_inc(c.s_wm.h, 1)
            return c.s_wm.inc(1)

        for i in range(3):
            atanh_col(i, arow[0:1, i:i + 1])
        hrow = nc.dram_tensor("wm_hrow", [1, 8], F32, kind="Internal").ap()
        nc.gpsimd.wait_ge(c.s_wm.h, c.s_wm.v)
        dma_group(c, nc.gpsimd, c.d["ca3"],
                  [(hrow[0:1, 0:3], srow[0:1, 3:6]),
                   (hrow[0:1, 3:6], arow[0:1, 0:3])])
        nc.gpsimd.wait_ge(c.d["ca3"].h, c.d["ca3"].v)
        pairs = []
        for i, (tdst, adst) in enumerate(
                [(c.thr1, c.acol1), (c.thr2, c.acol2), (c.thr3, c.acol3)]):
            srct = bass.AP(hrow.tensor, hrow.offset + i, [[0, 128], [1, 1]])
            srca = bass.AP(hrow.tensor, hrow.offset + 3 + i, [[0, 128], [1, 1]])
            pairs += [(tdst[:], srct), (adst[:], srca)]
        bv = dma_group(c, nc.gpsimd, c.d["bc0"], pairs)
        nc.vector.wait_ge(c.d["bc0"].h, bv)
        for tsrc, ndst in [(c.thr1, c.nthr1), (c.thr2, c.nthr2), (c.thr3, c.nthr3)]:
            va(c, nc.vector.tensor_scalar_mul(ndst[:], tsrc[:], -1.0))


    c.wmean_finish = wmean_finish

def phase_proj_spill(c, WT, bvec, ncols, dst, sigmoid):
    nc = c.nc
    ntc = c.ntc
    es2 = ExitStack()
    tag = WT.tensor.name
    n_os = ncols // 512
    wt_f32 = es2.enter_context(nc.sbuf_tensor(tag + "_wf", [128, 2, 4 * 512], F32))
    wt_h = es2.enter_context(nc.sbuf_tensor(tag + "_wh", [128, NDC * 512], BF16))
    wt_l = es2.enter_context(nc.sbuf_tensor(tag + "_wl", [128, NDC * 512], BF16))
    xf_t = es2.enter_context(nc.sbuf_tensor(tag + "_xf", [128, 2, 2, DM], BF16))
    bbc = es2.enter_context(nc.sbuf_tensor(tag + "_bbc", [128, 512], F32))
    o_t = es2.enter_context(nc.sbuf_tensor(tag + "_o", [128, 2, 512], F32))
    ps = es2.enter_context(nc.psum_tensor(tag + "_ps", [128, 2, 512], F32))

    ld_sems = [c.d["ld0"], c.d["ld1"]]
    st_sems = [c.d["st0"], c.d["st1"]]
    xf_free = [None, None]
    o_free = [None, None]
    ps_free = [None, None]

    for os_ in range(n_os):
        nc.sync.wait_ge(c.s_pe.h, c.s_pe.v)
        w_split = load_split_pipelined(c, WT, os_ * 512, 512, wt_h, wt_l,
                                       wt_f32, c.d["wt0"], c.d["ca0"])
        nc.gpsimd.wait_ge(c.s_dve.h, c.s_dve.v)
        b_ready = dma_bcast_row(c, nc.gpsimd, bbc[:],
                                bvec[0:1, os_ * 512:(os_ + 1) * 512], c.d["bc0"])

        pp_ld = [None, None]

        def pp_load(tc):
            si = tc % 2
            if xf_free[si] is not None:
                nc.sync.wait_ge(c.s_pe.h, xf_free[si])
            pp_ld[si] = dma_group(c, nc.sync, ld_sems[si],
                                  [(xf_t[:, si, 0], c.xnfm_h[tc]),
                                   (xf_t[:, si, 1], c.xnfm_l[tc])])

        pp_load(0)
        for tc in range(ntc):
            si = tc % 2
            if tc + 1 < ntc:
                pp_load(tc + 1)
            pj = tc % 2
            if ps_free[pj] is not None:
                nc.tensor.wait_ge(c.s_dve.h, ps_free[pj])
            nc.tensor.wait_ge(ld_sems[si].h, pp_ld[si])
            if tc == 0:
                nc.tensor.wait_ge(c.s_dve.h, w_split)
            n = 0
            inst = None
            for dc in range(NDC):
                for lhs, rhs in ((0, wt_h), (1, wt_h), (0, wt_l)):
                    inst = nc.tensor.matmul(
                        ps[:, pj],
                        xf_t[:, si, lhs, dc * 128:(dc + 1) * 128],
                        rhs[:, dc * 512:(dc + 1) * 512],
                        start=(n == 0), stop=(n == NDC * 3 - 1))
                    n += 1
            inst.then_inc(c.s_pe.h, 1)
            mm_done = c.s_pe.inc(1)
            xf_free[si] = mm_done
            oj = tc % 2
            if o_free[oj] is not None:
                nc.vector.wait_ge(st_sems[oj].h, o_free[oj])
            nc.vector.wait_ge(c.s_pe.h, mm_done)
            if tc == 0:
                nc.vector.wait_ge(c.d["bc0"].h, b_ready)
            nc.vector.tensor_add(o_t[:, oj], ps[:, pj], bbc[:]).then_inc(c.s_dve.h, 1)
            add_done = c.s_dve.inc(1)
            ps_free[pj] = add_done
            if sigmoid:
                nc.scalar.wait_ge(c.s_dve.h, add_done)
                nc.scalar.activation(o_t[:, oj], o_t[:, oj],
                                     AF.Sigmoid).then_inc(c.s_act.h, 1)
                ev_done = c.s_act.inc(1)
                nc.sync.wait_ge(c.s_act.h, ev_done)
            else:
                nc.sync.wait_ge(c.s_dve.h, add_done)
            o_free[oj] = dma(c, nc.sync, st_sems[oj],
                             dst[tc][:, os_ * 512:(os_ + 1) * 512], o_t[:, oj])

    for s in ("st0", "st1"):
        nc.sync.wait_ge(c.d[s].h, c.d[s].v)
    es2.close()


def phase_kside(c):
    nc = c.nc
    ntc = c.ntc
    es2 = ExitStack()
    wk_f = es2.enter_context(nc.sbuf_tensor("ks_wf", [128, 2, 2 * 512], F32))
    wk_h = es2.enter_context(nc.sbuf_tensor("ks_wh", [128, NDC * 512], BF16))
    wk_l = es2.enter_context(nc.sbuf_tensor("ks_wl", [128, NDC * 512], BF16))
    xf_t = es2.enter_context(nc.sbuf_tensor("ks_xf", [128, 2, 2, DM], BF16))
    bkbc = es2.enter_context(nc.sbuf_tensor("ks_bk", [128, 512], F32))
    m4f = es2.enter_context(nc.sbuf_tensor("ks_m4f", [128, 512], F32))
    m4h = es2.enter_context(nc.sbuf_tensor("ks_m4h", [128, HKV * NFC * 512], BF16))
    m4l = es2.enter_context(nc.sbuf_tensor("ks_m4l", [128, HKV * NFC * 512], BF16))
    nrf = es2.enter_context(nc.sbuf_tensor("ks_nrf", [128, HKV * NFC * 4], F32))
    nrh = es2.enter_context(nc.sbuf_tensor("ks_nrh", [128, HKV * NFC * 4], BF16))
    nrl = es2.enter_context(nc.sbuf_tensor("ks_nrl", [128, HKV * NFC * 4], BF16))
    k_tm = es2.enter_context(nc.sbuf_tensor("ks_k", [128, HKV * DH], F32))
    x2e = es2.enter_context(nc.sbuf_tensor("ks_x2e", [128, HKV * 259], F32))
    u_tm = es2.enter_context(nc.sbuf_tensor("ks_u", [128, 2, HKV * FD], F32))
    u_h = es2.enter_context(nc.sbuf_tensor("ks_uh", [128, 2, HKV * FD], BF16))
    u_l = es2.enter_context(nc.sbuf_tensor("ks_ul", [128, 2, HKV * FD], BF16))
    ufm_h = es2.enter_context(nc.sbuf_tensor("ks_ufh", [128, HKV * FD], BF16))
    ufm_l = es2.enter_context(nc.sbuf_tensor("ks_ufl", [128, HKV * FD], BF16))
    sqs = es2.enter_context(nc.sbuf_tensor("ks_sqs", [128, FD], F32))
    v_t = es2.enter_context(nc.sbuf_tensor("ks_v", [128, 2, HKV * DH], F32))
    mb_g = es2.enter_context(nc.sbuf_tensor("ks_mbg", [128, 2, 512], F32))
    wmv_t = es2.enter_context(nc.sbuf_tensor("ks_wmv", [128, DM], F32))
    wmv_h = es2.enter_context(nc.sbuf_tensor("ks_wmvh", [128, DM], BF16))
    wmv_l = es2.enter_context(nc.sbuf_tensor("ks_wmvl", [128, DM], BF16))
    kr = es2.enter_context(nc.sbuf_tensor("ks_kr", [128, 2, 24], F32))
    den_sc = es2.enter_context(nc.sbuf_tensor("ks_dsc", [128, 4], F32))
    den_rc = es2.enter_context(nc.sbuf_tensor("ks_drc", [128, 4], F32))
    cf_sc = es2.enter_context(nc.sbuf_tensor("ks_cfs", [128, 4], F32))
    ps_k = es2.enter_context(nc.psum_tensor("ks_psk", [128, 2, 512], F32))
    ps_num = es2.enter_context(nc.psum_tensor("ks_psn", [128, 2, 512], F32))
    ps_den = es2.enter_context(nc.psum_tensor("ks_psd", [128, 2, 512], F32))
    ps_tr = es2.enter_context(nc.psum_tensor("ks_pstr", [128, 2, 1024], BF16))

    w_split = load_split_pipelined(c, c.WkT, 0, 512, wk_h, wk_l, wk_f,
                                   c.d["wt0"], c.d["ca0"], chunk_dc=2)
    b_ready = dma_bcast_row(c, nc.gpsimd, bkbc[:], c.bk[0:1, :], c.d["bc0"])
    for g in range(G):
        for fc in range(NFC):
            t0 = (g * NFC + fc) * 512
            nc.sync.wait_ge(c.d["ca1"].h, c.d["ca1"].v)
            nc.sync.wait_ge(c.s_dve.h, c.s_dve.v)
            mld = dma_group(c, nc.sync, c.d["wt1"],
                            [(m4f[:, j * 128:(j + 1) * 128],
                              c.mem[4 * g + j, fc * 128:(fc + 1) * 128, :])
                             for j in range(G)])
            nc.gpsimd.wait_ge(c.d["wt1"].h, mld)
            mh_done = dma(c, nc.gpsimd, c.d["ca1"], m4h[:, t0:t0 + 512], m4f[:])
            nc.vector.wait_ge(c.d["ca1"].h, mh_done)
            nc.vector.tensor_sub(m4l[:, t0:t0 + 512], m4f[:],
                                 m4h[:, t0:t0 + 512]).then_inc(c.s_dve.h, 1)
            c.s_dve.inc(1)
    nr_ld = dma_group(c, nc.sync, c.d["wt1"],
                      [(nrf[:, (g * NFC + fc) * 4 + j:(g * NFC + fc) * 4 + j + 1],
                        c.nrm[4 * g + j, fc * 128:(fc + 1) * 128]
                        .rearrange("(f o) -> f o", o=1))
                       for g in range(G) for fc in range(NFC) for j in range(G)])
    nc.gpsimd.wait_ge(c.d["wt1"].h, nr_ld)
    nrh_done = dma(c, nc.gpsimd, c.d["ca1"], nrh[:], nrf[:])
    nc.vector.wait_ge(c.d["ca1"].h, nrh_done)
    nc.vector.tensor_sub(nrl[:], nrf[:], nrh[:]).then_inc(c.s_dve.h, 1)
    consts_done = c.s_dve.inc(1)

    ld_sems = [c.d["ld0"], c.d["ld1"]]
    vm_sems = [c.d["ld2"], c.d["ld3"]]
    mb_sems = [c.d["wt0"], c.d["wt1"]]
    ca_sems = [c.d["ca2"], c.d["ca3"]]
    tr_free = [None, None]
    numden_free = [None, None]
    lane = [dict(), dict()]

    def ks_load(tc):
        L = tc % 2
        lk = lane[L]
        if "mm" in lk:
            nc.sync.wait_ge(c.s_pe.h, lk["mm"])
        lk["xld"] = dma_group(c, nc.sync, ld_sems[L],
                              [(xf_t[:, L, 0], c.xnfm_h[tc]),
                               (xf_t[:, L, 1], c.xnfm_l[tc])])
        if "vmread" in lk:
            nc.sync.wait_ge(c.s_dve.h, lk["vmread"])
        lk["vld"] = dma(c, nc.sync, vm_sems[L], v_t[:, L], c.v_s[tc])

    def stage_a(tc):
        L = tc % 2
        lk = lane[L]
        other = lane[1 - L]
        nc.tensor.wait_ge(ld_sems[L].h, lk["xld"])
        if tc <= 1:
            nc.tensor.wait_ge(c.s_dve.h, w_split)
        if "ke" in lk:
            nc.tensor.wait_ge(c.s_dve.h, lk["ke"])
        n = 0
        inst = None
        for dc in range(NDC):
            for lhs, rhs in ((0, wk_h), (1, wk_h), (0, wk_l)):
                inst = nc.tensor.matmul(
                    ps_k[:, L], xf_t[:, L, lhs, dc * 128:(dc + 1) * 128],
                    rhs[:, dc * 512:(dc + 1) * 512],
                    start=(n == 0), stop=(n == NDC * 3 - 1))
                n += 1
        inst.then_inc(c.s_pe.h, 1)
        lk["mm"] = c.s_pe.inc(1)
        nc.vector.wait_ge(c.s_pe.h, lk["mm"])
        if tc <= 1:
            nc.vector.wait_ge(c.d["bc0"].h, b_ready)
        if "x2" in lk:
            nc.vector.wait_ge(c.s_act.h, lk["x2"])
        if "x2" in other:
            nc.vector.wait_ge(c.s_act.h, other["x2"])
        nc.vector.tensor_add(k_tm[:], ps_k[:, L], bkbc[:]).then_inc(c.s_dve.h, 1)
        lk["ke"] = c.s_dve.inc(1)
        nc.scalar.wait_ge(c.s_dve.h, lk["ke"])
        if "u" in lk:
            nc.scalar.wait_ge(c.s_dve.h, lk["u"])
        if "u" in other:
            nc.scalar.wait_ge(c.s_dve.h, other["u"])
        inst = None
        for kv in range(HKV):
            b0 = kv * 259
            inst = nc.scalar.activation(x2e[:, b0 + 3:b0 + 131],
                                        k_tm[:, kv * 128:(kv + 1) * 128], AF.Relu)
            inst = nc.scalar.activation(x2e[:, b0 + 131:b0 + 259],
                                        k_tm[:, kv * 128:(kv + 1) * 128], AF.Relu,
                                        scale=-1.0)
        inst.then_inc(c.s_act.h, 1)
        lk["x2"] = c.s_act.inc(1)
        nc.vector.wait_ge(c.s_act.h, lk["x2"])
        inst = None
        for kv in range(HKV):
            b0 = kv * 259
            inst = nc.vector.tensor_copy(x2e[:, b0:b0 + 3],
                                         x2e[:, b0 + 256:b0 + 259])
        inst.then_inc(c.s_dve.h, 1)
        nc.vector.wait_ge(c.s_dve.h, c.s_dve.inc(1))
        if "sq" in lk:
            nc.vector.wait_ge(c.s_act.h, lk["sq"])
        if "uhi" in lk:
            nc.vector.wait_ge(ca_sems[L].h, lk["uhi"])
        inst = None
        for kv in range(HKV):
            b0 = kv * 259
            for jb in range(NU):
                o0 = kv * FD + jb * 256
                inst = nc.vector.tensor_mul(
                    u_tm[:, L, o0:o0 + 256],
                    x2e[:, b0 + 3:b0 + 259],
                    x2e[:, b0 + 2 - jb:b0 + 258 - jb])
        inst.then_inc(c.s_dve.h, 1)
        lk["u"] = c.s_dve.inc(1)
        nc.scalar.wait_ge(c.s_dve.h, lk["u"])
        if "rows" in lk:
            nc.scalar.wait_ge(c.s_dve.h, lk["rows"])
        for kv in range(HKV):
            af(c, nc.scalar.activation(sqs[:], u_tm[:, L, kv * FD:(kv + 1) * FD],
                                       AF.Square, accum_out=kr[:, L, kv:kv + 1]))
        nc.scalar.activation(kr[:, L, 4:8], kr[:, L, 0:4],
                             AF.Sqrt).then_inc(c.s_act.h, 1)
        lk["sqrt"] = c.s_act.inc(1)
        nc.vector.wait_ge(c.s_act.h, lk["sqrt"])
        vf(c, nc.vector.tensor_scalar_max(kr[:, L, 4:8], kr[:, L, 4:8], L2_EPS))
        vf(c, nc.vector.reciprocal(kr[:, L, 8:12], kr[:, L, 4:8]))
        vf(c, nc.vector.tensor_scalar_mul(kr[:, L, 12:16], kr[:, L, 4:8], EPS))
        vf(c, nc.vector.tensor_scalar_mul(kr[:, L, 16:20], kr[:, L, 8:12], -1.0))
        vf(c, nc.vector.tensor_scalar_max(kr[:, L, 20:24], kr[:, L, 0:4], 1e-30))
        vf(c, nc.vector.reciprocal(kr[:, L, 20:24], kr[:, L, 20:24]))
        lk["rows"] = c.s_dve.v
        lk["sq"] = lk["sqrt"]
        nc.gpsimd.wait_ge(c.s_dve.h, lk["u"])
        if "tr" in lk:
            nc.gpsimd.wait_ge(c.s_act.h, lk["tr"])
        if "usp" in lk:
            nc.gpsimd.wait_ge(c.d["st2"].h, lk["usp"])
        lk["uhi"] = dma(c, nc.gpsimd, ca_sems[L], u_h[:, L], u_tm[:, L])
        nc.vector.wait_ge(ca_sems[L].h, lk["uhi"])
        if "usp" in lk:
            nc.vector.wait_ge(c.d["st2"].h, lk["usp"])
        nc.vector.tensor_sub(u_l[:, L], u_tm[:, L],
                             u_h[:, L]).then_inc(c.s_dve.h, 1)
        lk["ulo"] = c.s_dve.inc(1)
        nc.sync.wait_ge(ca_sems[L].h, lk["uhi"])
        nc.sync.wait_ge(c.s_dve.h, lk["ulo"])
        lk["usp"] = dma_group(c, nc.sync, c.d["st2"],
                              [(c.ukh_s[tc], u_h[:, L]), (c.ukl_s[tc], u_l[:, L])])

    def stage_b(tc):
        L = tc % 2
        lk = lane[L]
        other = lane[1 - L]
        if "numlast" in other:
            nc.scalar.wait_ge(c.s_pe.h, other["numlast"])
        nc.tensor.wait_ge(ca_sems[L].h, lk["uhi"])
        transpose_tiles(c, u_h[:, L], HKV * NFC, ufm_h[:], ps_tr, tr_free)
        nc.tensor.wait_ge(c.s_dve.h, lk["ulo"])
        transpose_tiles(c, u_l[:, L], HKV * NFC, ufm_l[:], ps_tr, tr_free)
        lk["tr"] = c.s_act.v
        nc.vector.wait_ge(vm_sems[L].h, lk["vld"])
        mb_ld = [None, None]
        for g in range(G):
            ms = g % 2
            if g == 0:
                for gg in (0, 1):
                    nc.sync.wait_ge(c.s_dve.h, c.s_dve.v)
                    mb_ld[gg] = dma(c, nc.sync, mb_sems[gg], mb_g[:, gg],
                                    c.mb_s[tc][:, gg * 512:(gg + 1) * 512])
            elif g < G - 1:
                nc.sync.wait_ge(c.s_dve.h, c.s_dve.v)
                mb_ld[(g + 1) % 2] = dma(c, nc.sync, mb_sems[(g + 1) % 2],
                                         mb_g[:, (g + 1) % 2],
                                         c.mb_s[tc][:, (g + 1) * 512:(g + 2) * 512])
            pj = g % 2
            if numden_free[pj] is not None:
                nc.tensor.wait_ge(c.s_dve.h, numden_free[pj])
            nc.tensor.wait_ge(c.s_act.h, lk["tr"])
            if tc <= 1 and g == 0:
                nc.tensor.wait_ge(c.s_dve.h, consts_done)
            n = 0
            inst = None
            for fc in range(NFC):
                s0 = (g * NFC + fc) * 128
                t0 = (g * NFC + fc) * 512
                for lhs, rhs in ((ufm_h, m4h), (ufm_h, m4l), (ufm_l, m4h)):
                    inst = nc.tensor.matmul(
                        ps_num[:, pj], lhs[:, s0:s0 + 128],
                        rhs[:, t0:t0 + 512],
                        start=(n == 0), stop=(n == NFC * 3 - 1))
                    n += 1
            n = 0
            for fc in range(NFC):
                s0 = (g * NFC + fc) * 128
                t0 = (g * NFC + fc) * 4
                for lhs, rhs in ((ufm_h, nrh), (ufm_h, nrl), (ufm_l, nrh)):
                    inst = nc.tensor.matmul(
                        ps_den[:, pj, 0:4], lhs[:, s0:s0 + 128],
                        rhs[:, t0:t0 + 4],
                        start=(n == 0), stop=(n == NFC * 3 - 1))
                    n += 1
            inst.then_inc(c.s_pe.h, 1)
            nd_done = c.s_pe.inc(1)
            lk["numlast"] = nd_done
            nc.vector.wait_ge(c.s_pe.h, nd_done)
            vf(c, nc.vector.tensor_scalar(den_sc[:, 0:4],
                                          ps_den[:, pj, 0:4],
                                          kr[:, L, 12 + g:13 + g], None, ALU.add))
            vf(c, nc.vector.reciprocal(den_rc[:, 0:4], den_sc[:, 0:4]))
            vf(c, nc.vector.tensor_scalar(cf_sc[:, 0:4], den_sc[:, 0:4],
                                          kr[:, L, 4 + g:5 + g], None, ALU.mult))
            vf(c, nc.vector.tensor_scalar(cf_sc[:, 0:4], cf_sc[:, 0:4],
                                          kr[:, L, 20 + g:21 + g], None, ALU.mult))
            vf(c, nc.vector.tensor_scalar(cf_sc[:, 0:4], cf_sc[:, 0:4], 1.0,
                                          None, ALU.subtract))
            vf(c, nc.vector.tensor_scalar_mul(cf_sc[:, 0:4], cf_sc[:, 0:4], -1.0))
            vf(c, nc.vector.tensor_scalar_max(cf_sc[:, 0:4], cf_sc[:, 0:4], 0.0))
            vf(c, nc.vector.tensor_scalar_min(cf_sc[:, 0:4], cf_sc[:, 0:4], 1.0))
            vf(c, nc.vector.tensor_scalar(
                c.r_coef[:, tc * H + 4 * g:tc * H + 4 * g + 4],
                cf_sc[:, 0:4], kr[:, L, 8 + g:9 + g], None, ALU.mult))
            nc.vector.wait_ge(mb_sems[ms].h, mb_ld[ms])
            for j in range(G):
                h = 4 * g + j
                vf(c, nc.vector.scalar_tensor_tensor(
                    wmv_t[:, h * 128:(h + 1) * 128],
                    ps_num[:, pj, j * 128:(j + 1) * 128],
                    den_rc[:, j:j + 1],
                    v_t[:, L, g * 128:(g + 1) * 128],
                    ALU.mult, ALU.subtract))
                vf(c, nc.vector.tensor_mul(wmv_t[:, h * 128:(h + 1) * 128],
                                           wmv_t[:, h * 128:(h + 1) * 128],
                                           mb_g[:, ms, j * 128:(j + 1) * 128]))
                vf(c, nc.vector.tensor_scalar(wmv_t[:, h * 128:(h + 1) * 128],
                                              wmv_t[:, h * 128:(h + 1) * 128],
                                              kr[:, L, 16 + g:17 + g], None,
                                              ALU.mult))
            numden_free[pj] = c.s_dve.v
        lk["vmread"] = c.s_dve.v
        nc.gpsimd.wait_ge(c.s_dve.h, c.s_dve.v)
        nc.gpsimd.wait_ge(c.d["st3"].h, c.d["st3"].v)
        nc.vector.wait_ge(c.d["st3"].h, c.d["st3"].v)
        wm_hi, wm_split = hilo_split(c, wmv_t[:], wmv_h[:], wmv_l[:], c.d["ca1"])
        nc.sync.wait_ge(c.d["ca1"].h, wm_hi)
        nc.sync.wait_ge(c.s_dve.h, wm_split)
        dma_group(c, nc.sync, c.d["st3"],
                  [(c.wmvh_s[tc], wmv_h[:]), (c.wmvl_s[tc], wmv_l[:])])

    ks_load(0)
    for i in range(ntc + 1):
        if i + 1 < ntc:
            ks_load(i + 1)
        if i < ntc:
            stage_a(i)
        if i >= 1:
            stage_b(i - 1)

    if "rowdbg" in c.debug_names:
        nc.sync.wait_ge(c.s_dve.h, c.s_dve.v)
        dma(c, nc.sync, c.d["dbg"], c.rowdbg[:, 0:c.ntc * H if c.ntc * H <= 64 else 64],
            c.r_coef[:, 0:c.ntc * H if c.ntc * H <= 64 else 64])
    for s in ("st2", "st3", "dbg"):
        nc.sync.wait_ge(c.d[s].h, c.d[s].v)
    es2.close()


def phase_delta(c):
    nc = c.nc
    ntc = c.ntc
    es2 = ExitStack()
    usl = es2.enter_context(nc.sbuf_tensor("dl_u", [128, 2, 2, FD], BF16))
    cfh = es2.enter_context(nc.sbuf_tensor("dl_cfh", [128, ntc * H], BF16))
    cfl = es2.enter_context(nc.sbuf_tensor("dl_cfl", [128, ntc * H], BF16))
    wsl = es2.enter_context(nc.sbuf_tensor("dl_w", [128, 2, 2, 512], BF16))
    cfb = es2.enter_context(nc.sbuf_tensor("dl_cf", [128, 2, 2, 4], BF16))
    stg = es2.enter_context(nc.sbuf_tensor("dl_stg", [128, 512], F32))
    stgn = es2.enter_context(nc.sbuf_tensor("dl_stgn", [128, 24], F32))
    ps_d = es2.enter_context(nc.psum_tensor("dl_psd", [128, 6, 512], F32))
    ps_n = es2.enter_context(nc.psum_tensor("dl_psn", [128, 24], F32))

    ld_sems = [c.d["ld0"], c.d["ld1"]]
    lw_sems = [c.d["ld2"], c.d["ld3"]]
    slot_mm = [None, None]

    vf(c, nc.vector.tensor_copy(cfh[:], c.r_coef[:, 0:ntc * H]))
    nc.vector.tensor_sub(cfl[:], c.r_coef[:, 0:ntc * H],
                         cfh[:]).then_inc(c.s_dve.h, 1)
    coef_split = c.s_dve.inc(1)

    for g in range(G):
        for tc in range(ntc):
            si = tc % 2
            if slot_mm[si] is not None:
                nc.sync.wait_ge(c.s_pe.h, slot_mm[si])
            u_ld = dma_group(c, nc.sync, ld_sems[si],
                             [(usl[:, si, 0], c.ukh_s[tc][:, g * FD:(g + 1) * FD]),
                              (usl[:, si, 1], c.ukl_s[tc][:, g * FD:(g + 1) * FD])])
            w_ld = dma_group(c, nc.sync, lw_sems[si],
                             [(wsl[:, si, 0], c.wmvh_s[tc][:, g * 512:(g + 1) * 512]),
                              (wsl[:, si, 1], c.wmvl_s[tc][:, g * 512:(g + 1) * 512])])
            nc.tensor.wait_ge(ld_sems[si].h, u_ld)
            nc.tensor.wait_ge(lw_sems[si].h, w_ld)
            if g == 0 and tc == 0:
                nc.tensor.wait_ge(c.s_dve.h, coef_split)
            if tc == 0:
                nc.tensor.wait_ge(c.s_dve.h, c.s_dve.v)
            inst = None
            first = tc == 0
            last = tc == ntc - 1
            for fc in range(NFC):
                for ci_, (lhs, rhs) in enumerate(((0, 0), (0, 1), (1, 0))):
                    inst = nc.tensor.matmul(
                        ps_d[:, fc],
                        usl[:, si, lhs, fc * 128:(fc + 1) * 128],
                        wsl[:, si, rhs],
                        start=(first and ci_ == 0),
                        stop=(last and ci_ == 2))
            for fc in range(NFC):
                for ci_, (lhs, rhs) in enumerate(((0, 0), (0, 1), (1, 0))):
                    cfsrc = (cfh if rhs == 0 else cfl)
                    inst = nc.tensor.matmul(
                        ps_n[:, fc * 4:(fc + 1) * 4],
                        usl[:, si, lhs, fc * 128:(fc + 1) * 128],
                        cfsrc[:, tc * H + 4 * g:tc * H + 4 * g + 4],
                        start=(first and fc == 0 and ci_ == 0),
                        stop=(last and fc == NFC - 1 and ci_ == 2))
            inst.then_inc(c.s_pe.h, 1)
            slot_mm[si] = c.s_pe.inc(1)

        nc.vector.wait_ge(c.s_pe.h, c.s_pe.v)
        for fc in range(NFC):
            nc.vector.wait_ge(c.d["st0"].h, c.d["st0"].v)
            vf(c, nc.vector.tensor_copy(stg[:], ps_d[:, fc]))
            nc.sync.wait_ge(c.s_dve.h, c.s_dve.v)
            dma(c, nc.sync, c.d["st0"], c.cc_m[g, fc], stg[:])
        nc.vector.wait_ge(c.d["st1"].h, c.d["st1"].v)
        vf(c, nc.vector.tensor_copy(stgn[:], ps_n[:]))
        nc.sync.wait_ge(c.s_dve.h, c.s_dve.v)
        dma(c, nc.sync, c.d["st1"], c.cc_n[g], stgn[:])

    for s in ("st0", "st1"):
        nc.sync.wait_ge(c.d[s].h, c.d[s].v)
    if c.use_cc:
        nc.gpsimd.wait_ge(c.d["st0"].h, c.d["st0"].v)
        nc.gpsimd.wait_ge(c.d["st1"].h, c.d["st1"].v)
        nc.gpsimd.collective_compute(
            "AllReduce", ALU.add, replica_groups=[list(range(NCORES))],
            ins=[c.cc_m.opt()], outs=[c.cc_m.opt()],
        ).then_inc(c.s_ccs.h, 1)
        nc.gpsimd.collective_compute(
            "AllReduce", ALU.add, replica_groups=[list(range(NCORES))],
            ins=[c.cc_n.opt()], outs=[c.cc_n.opt()],
        ).then_inc(c.s_ccs.h, 1)
        c.s_ccs.inc(2)
    es2.close()


def ternarize(c, wsrc_f32, nfree, sa, sb, dst_bf16, thr, nthr):
    nc = c.nc
    nc.scalar.wait_ge(c.s_dve.h, c.s_dve.v)
    af(c, nc.scalar.activation(sa[:, 0:nfree], wsrc_f32, AF.Sign, bias=nthr[:]))
    nc.scalar.activation(sb[:, 0:nfree], wsrc_f32, AF.Sign,
                         bias=thr[:]).then_inc(c.s_act.h, 1)
    s_done = c.s_act.inc(1)
    nc.vector.wait_ge(c.s_act.h, s_done)
    nc.vector.tensor_add(dst_bf16, sa[:, 0:nfree], sb[:, 0:nfree]).then_inc(c.s_dve.h, 1)
    return c.s_dve.inc(1)


def phase_ffn(c):
    nc = c.nc
    ntc = c.ntc
    es0 = ExitStack()
    c.alpha1 = es0.enter_context(nc.sbuf_tensor("ff_al1", [128, ntc], F32))
    c.alpha2 = es0.enter_context(nc.sbuf_tensor("ff_al2", [128, ntc], F32))
    for (r_sc, acol, adst) in ((c.r_sc1, c.acol1, c.alpha1),
                               (c.r_sc2, c.acol2, c.alpha2)):
        vf(c, nc.vector.reciprocal(adst[:], r_sc[:]))
        vf(c, nc.vector.tensor_scalar(adst[:], adst[:], acol[:], None, ALU.mult))
        vf(c, nc.vector.tensor_scalar_mul(adst[:], adst[:], 0.5))

    es2 = ExitStack()
    w_f = es2.enter_context(nc.sbuf_tensor("ff_wf", [128, NDC * 512], F32))
    sa = es2.enter_context(nc.sbuf_tensor("ff_sa", [128, 4096], BF16))
    sb = es2.enter_context(nc.sbuf_tensor("ff_sb", [128, 4096], BF16))
    t2 = es2.enter_context(nc.sbuf_tensor("ff_t2", [128, 2, 2, 2, NDC * 512], BF16))
    kf = es2.enter_context(nc.sbuf_tensor("ff_kf", [128, 2, 2, DM], BF16))
    sw = es2.enter_context(nc.sbuf_tensor("ff_sw", [128, 2, 512], F32))
    hst = es2.enter_context(nc.sbuf_tensor("ff_h", [128, 2, 512], F32))
    ps1 = es2.enter_context(nc.psum_tensor("ff_ps1", [128, 2, 512], F32))
    ps2 = es2.enter_context(nc.psum_tensor("ff_ps2", [128, 2, 512], F32))

    ld_sems = [c.d["ld0"], c.d["ld1"]]
    st_sems = [c.d["st0"], c.d["st3"]]
    kf_free = [None, None]
    h_free = [None, None]
    sw_free = [None, None]
    ps1_free = [None, None]
    ps2_free = [None, None]
    t2_free = [None, None]
    n_os = FF // 512

    def ternarize_piece(op, tslot, piece):
        mi, oi, half = piece // 8, (piece // 4) % 2, piece % 4
        WT, thr, nthr = ((c.w1T, c.thr1, c.nthr1),
                         (c.w2T, c.thr2, c.nthr2))[mi]
        os_ = op * 2 + oi
        nc.sync.wait_ge(c.s_act.h, c.s_act.v)
        w_ld = dma_group(c, nc.sync, c.d["wt0"],
                         [(w_f[:, i * 512:(i + 1) * 512],
                           WT[(half * 4 + i) * 128:(half * 4 + i + 1) * 128,
                              os_ * 512:(os_ + 1) * 512])
                          for i in range(4)])
        nc.scalar.wait_ge(c.d["wt0"].h, w_ld)
        if t2_free[tslot] is not None and piece == 0:
            nc.scalar.wait_ge(c.s_pe.h, t2_free[tslot])
            nc.vector.wait_ge(c.s_pe.h, t2_free[tslot])
        ternarize(c, w_f[:, 0:4 * 512], 4 * 512, sa, sb,
                  t2[:, tslot, mi, oi, half * 2048:(half + 1) * 2048],
                  thr, nthr)
        return c.s_dve.v

    t2_done = [None, None]
    for piece in range(16):
        t2_done[0] = ternarize_piece(0, 0, piece)

    for op in range(n_os // 2):
        tslot = op % 2
        kf_ld = [None, None]

        def ffn_kload(tc):
            si = tc % 2
            if kf_free[si] is not None:
                nc.sync.wait_ge(c.s_pe.h, kf_free[si])
            kf_ld[si] = dma_group(c, nc.sync, ld_sems[si],
                                  [(kf[:, si, 0], c.k1fm[tc]),
                                   (kf[:, si, 1], c.k2fm[tc])])

        ffn_kload(0)
        for tc in range(ntc):
            si = tc % 2
            if tc + 1 < ntc:
                ffn_kload(tc + 1)
            for oi in range(2):
                pj = (tc * 2 + oi) % 2
                nc.tensor.wait_ge(ld_sems[si].h, kf_ld[si])
                if tc == 0 and oi == 0:
                    nc.tensor.wait_ge(c.s_dve.h, t2_done[tslot])
                if ps1_free[pj] is not None:
                    nc.tensor.wait_ge(c.s_act.h, ps1_free[pj])
                if ps2_free[pj] is not None:
                    nc.tensor.wait_ge(c.s_dve.h, ps2_free[pj])
                inst = None
                for mi, pst in ((0, ps1), (1, ps2)):
                    for dc in range(NDC):
                        inst = nc.tensor.matmul(
                            pst[:, pj], kf[:, si, mi, dc * 128:(dc + 1) * 128],
                            t2[:, tslot, mi, oi, dc * 512:(dc + 1) * 512],
                            start=(dc == 0), stop=(dc == NDC - 1))
                inst.then_inc(c.s_pe.h, 1)
                mm_done = c.s_pe.inc(1)
                kf_free[si] = mm_done
                nc.scalar.wait_ge(c.s_pe.h, mm_done)
                if sw_free[pj] is not None:
                    nc.scalar.wait_ge(c.s_dve.h, sw_free[pj])
                nc.scalar.activation(sw[:, pj], ps1[:, pj], AF.Silu,
                                     scale=c.alpha1[:, tc:tc + 1]).then_inc(c.s_act.h, 1)
                sw_done = c.s_act.inc(1)
                ps1_free[pj] = sw_done
                if h_free[pj] is not None:
                    nc.vector.wait_ge(st_sems[pj].h, h_free[pj])
                nc.vector.wait_ge(c.s_act.h, sw_done)
                nc.vector.scalar_tensor_tensor(
                    hst[:, pj], ps2[:, pj], c.alpha2[:, tc:tc + 1], sw[:, pj],
                    ALU.mult, ALU.mult).then_inc(c.s_dve.h, 1)
                h_done = c.s_dve.inc(1)
                ps2_free[pj] = h_done
                sw_free[pj] = h_done
                os_ = op * 2 + oi
                nc.sync.wait_ge(c.s_dve.h, h_done)
                h_free[pj] = dma(c, nc.sync, st_sems[pj],
                                 c.h_s[tc][:, os_ * 512:(os_ + 1) * 512], hst[:, pj])
            if op + 1 < n_os // 2:
                if tc == 0 and oi == 0:
                    t2_free[(op + 1) % 2] = c.s_pe.v
                if oi == 1:
                    per = -(-16 // ntc)
                    for piece in range(tc * per, min((tc + 1) * per, 16)):
                        t2_done[(op + 1) % 2] = ternarize_piece(
                            op + 1, (op + 1) % 2, piece)
    for s in ("st0", "st3"):
        nc.sync.wait_ge(c.d[s].h, c.d[s].v)
    es2.close()
    nc.all_engine_barrier()

    es3 = ExitStack()
    QW = FF // 4
    h1q = es3.enter_context(nc.sbuf_tensor("fq_h1", [128, 2, QW], F32))
    h2q = es3.enter_context(nc.sbuf_tensor("fq_h2", [128, 2, QW], F32))
    hgq = es3.enter_context(nc.sbuf_tensor("fq_hg", [128, 2, QW], F32))
    phq = es3.enter_context(nc.sbuf_tensor("fq_ph", [128, 2, QW], F32))
    ciq = es3.enter_context(nc.sbuf_tensor("fq_ci", [128, 2, QW], I16))
    cfq = es3.enter_context(nc.sbuf_tensor("fq_cf", [128, 2, QW], BF16))
    k3_t = es3.enter_context(nc.sbuf_tensor("fq_k3", [128, 2, FF], BF16))
    stg = es3.enter_context(nc.sbuf_tensor("fq_stg", [128, FF], BF16))
    g3bc = es3.enter_context(nc.sbuf_tensor("fq_g3", [128, FF], F32))
    hgq2 = es3.enter_context(nc.sbuf_tensor("fq_hg2", [128, 2, QW], F32))
    rows = es3.enter_context(nc.sbuf_tensor("fq_rows", [128, 16], F32))
    sqsc = es3.enter_context(nc.sbuf_tensor("fq_sqs", [128, QW], F32))
    ps_tr = es3.enter_context(nc.psum_tensor("fq_pstr", [128, 2, 1024], BF16))
    g3_rdy = dma_bcast_row(c, nc.gpsimd, g3bc[:], c.g3[0:1, :], c.d["bc0"])
    tr_free = [None, None]
    l1_sems = [c.d["ld2"], c.d["ld3"]]
    l2_sems = [c.d["wt0"], c.d["wt1"]]
    ca_sems = [c.d["ca1"], c.d["ca2"]]
    p1state = [dict(), dict()]
    p2state = [dict(), dict()]
    rowstate = {}

    def p4c_pass1(tc, qr):
        s = qr % 2
        st = p1state[s]
        if "sq" in st:
            nc.sync.wait_ge(c.s_act.h, st["sq"])
        if "pm" in st:
            nc.sync.wait_ge(c.s_pl.h, st["pm"])
        ld = dma(c, nc.sync, l1_sems[s], h1q[:, s],
                 c.h_s[tc][:, qr * QW:(qr + 1) * QW])
        nc.scalar.wait_ge(l1_sems[s].h, ld)
        if tc >= 2:
            nc.scalar.wait_ge(c.s_dve.h, rowstate[tc % 2]["gam"])
        af(c, nc.scalar.activation(
            sqsc[:], h1q[:, s], AF.Square,
            accum_out=rows[:, (tc % 2) * 8 + qr:(tc % 2) * 8 + qr + 1]))
        st["sq"] = c.s_act.v
        nc.gpsimd.wait_ge(l1_sems[s].h, ld)
        if tc == 0 and qr == 0:
            nc.gpsimd.wait_ge(c.d["bc0"].h, g3_rdy)
        if "mx" in st:
            nc.gpsimd.wait_ge(c.s_dve.h, st["mx"])
        nc.gpsimd.tensor_mul(hgq[:, s], h1q[:, s],
                             g3bc[:, qr * QW:(qr + 1) * QW]).then_inc(c.s_pl.h, 1)
        st["pm"] = c.s_pl.inc(1)
        nc.vector.wait_ge(c.s_pl.h, st["pm"])
        nc.vector.reduce_max(rows[:, (tc % 2) * 8 + 4 + qr:(tc % 2) * 8 + 5 + qr],
                             hgq[:, s], AX.X,
                             apply_absolute_value=True).then_inc(c.s_dve.h, 1)
        st["mx"] = c.s_dve.inc(1)

    def p4c_rows(tc):
        R = (tc % 2) * 8
        st = rowstate.setdefault(tc % 2, {})
        nc.vector.wait_ge(c.s_act.h, p1state[0].get("sq", 0))
        nc.vector.wait_ge(c.s_act.h, p1state[1].get("sq", 0))
        nc.vector.wait_ge(c.s_dve.h, c.s_dve.v)
        vf(c, nc.vector.tensor_add(rows[:, R:R + 1], rows[:, R:R + 1],
                                   rows[:, R + 1:R + 2]))
        vf(c, nc.vector.tensor_add(rows[:, R + 2:R + 3], rows[:, R + 2:R + 3],
                                   rows[:, R + 3:R + 4]))
        vf(c, nc.vector.tensor_add(rows[:, R:R + 1], rows[:, R:R + 1],
                                   rows[:, R + 2:R + 3]))
        vf(c, nc.vector.tensor_max(rows[:, R + 4:R + 5], rows[:, R + 4:R + 5],
                                   rows[:, R + 5:R + 6]))
        vf(c, nc.vector.tensor_max(rows[:, R + 6:R + 7], rows[:, R + 6:R + 7],
                                   rows[:, R + 7:R + 8]))
        vf(c, nc.vector.tensor_max(rows[:, R + 4:R + 5], rows[:, R + 4:R + 5],
                                   rows[:, R + 6:R + 7]))
        vf(c, nc.vector.tensor_scalar_mul(rows[:, R + 1:R + 2], rows[:, R:R + 1],
                                          1.0 / FF))
        nc.vector.tensor_scalar_add(rows[:, R + 1:R + 2], rows[:, R + 1:R + 2],
                                    EPS).then_inc(c.s_dve.h, 1)
        v1 = c.s_dve.inc(1)
        nc.scalar.wait_ge(c.s_dve.h, v1)
        nc.scalar.activation(rows[:, R + 2:R + 3], rows[:, R + 1:R + 2],
                             AF.Sqrt).then_inc(c.s_act.h, 1)
        v2 = c.s_act.inc(1)
        nc.vector.wait_ge(c.s_act.h, v2)
        vf(c, nc.vector.reciprocal(rows[:, R + 2:R + 3], rows[:, R + 2:R + 3]))
        vf(c, nc.vector.tensor_mul(rows[:, R + 3:R + 4], rows[:, R + 2:R + 3],
                                   rows[:, R + 4:R + 5]))
        vf(c, nc.vector.tensor_scalar_max(rows[:, R + 3:R + 4],
                                          rows[:, R + 3:R + 4], Q_EPS))
        vf(c, nc.vector.reciprocal(rows[:, R + 5:R + 6], rows[:, R + 3:R + 4]))
        vf(c, nc.vector.tensor_scalar_mul(c.r_sc3[:, tc:tc + 1],
                                          rows[:, R + 5:R + 6], 127.0))
        nc.vector.tensor_mul(rows[:, R + 6:R + 7], c.r_sc3[:, tc:tc + 1],
                             rows[:, R + 2:R + 3]).then_inc(c.s_dve.h, 1)
        st["gam"] = c.s_dve.inc(1)

    def p4c_pass2(tc, qr):
        s = qr % 2
        L = tc % 2
        st = p2state[s]
        if "k" in st:
            nc.sync.wait_ge(c.s_dve.h, st["k"])
        if "hg" in st:
            nc.sync.wait_ge(c.s_pl.h, st["hg"])
        ld = dma(c, nc.sync, l2_sems[s], h2q[:, s],
                 c.h_s[tc][:, qr * QW:(qr + 1) * QW])
        nc.gpsimd.wait_ge(l2_sems[s].h, ld)
        if "phi" in st:
            nc.gpsimd.wait_ge(c.s_act.h, st["phi"])
        nc.gpsimd.tensor_mul(hgq2[:, s], h2q[:, s],
                             g3bc[:, qr * QW:(qr + 1) * QW]).then_inc(c.s_pl.h, 1)
        st["hg"] = c.s_pl.inc(1)
        nc.scalar.wait_ge(c.s_pl.h, st["hg"])
        nc.scalar.wait_ge(c.s_dve.h, rowstate[L]["gam"])
        if "m" in st:
            nc.scalar.wait_ge(c.s_dve.h, st["m"])
        nc.scalar.activation(phq[:, s], hgq2[:, s], AF.Identity,
                             bias=c.halfcol[:],
                             scale=rows[:, L * 8 + 6:L * 8 + 7]).then_inc(c.s_act.h, 1)
        st["phi"] = c.s_act.inc(1)
        nc.vector.wait_ge(c.s_act.h, st["phi"])
        if "k" in st:
            nc.vector.wait_ge(c.s_dve.h, st["k"])
        vf(c, nc.vector.tensor_copy(ciq[:, s], phq[:, s]))
        st["ci"] = c.s_dve.v
        nc.scalar.wait_ge(c.s_dve.h, st["ci"])
        if "k" in st:
            nc.scalar.wait_ge(c.s_dve.h, st["k"])
        nc.scalar.activation(cfq[:, s], ciq[:, s], AF.Copy).then_inc(c.s_act.h, 1)
        st["cf"] = c.s_act.inc(1)
        nc.vector.wait_ge(c.s_act.h, st["cf"])
        vf(c, nc.vector.tensor_tensor(h2q[:, s], phq[:, s], cfq[:, s], ALU.is_lt))
        st["m"] = c.s_dve.v
        klane = p2k.setdefault(L, {})
        if "tr" in klane:
            nc.vector.wait_ge(c.s_act.h, klane["tr"])
        nc.vector.tensor_sub(k3_t[:, L, qr * QW:(qr + 1) * QW],
                             cfq[:, s], h2q[:, s]).then_inc(c.s_dve.h, 1)
        st["k"] = c.s_dve.inc(1)
        klane["k" + str(qr)] = st["k"]

    p2k = {}

    def p4c_spill(tc):
        L = tc % 2
        klane = p2k[L]
        nc.tensor.wait_ge(c.d["st1"].h, c.d["st1"].v)
        for qr in range(4):
            nc.tensor.wait_ge(c.s_dve.h, klane["k" + str(qr)])
        transpose_tiles(c, k3_t[:, L], FF // 128, stg[:], ps_tr, tr_free)
        klane["tr"] = c.s_act.v
        nc.sync.wait_ge(c.s_act.h, klane["tr"])
        dma(c, nc.sync, c.d["st1"], c.k3fm[tc], stg[:])

    for qr in range(4):
        p4c_pass1(0, qr)
    for tc in range(ntc):
        p4c_rows(tc)
        for qr in range(4):
            p4c_pass2(tc, qr)
            if tc + 1 < ntc:
                p4c_pass1(tc + 1, qr)
        p4c_spill(tc)
    nc.sync.wait_ge(c.d["st1"].h, c.d["st1"].v)
    es3.close()
    nc.all_engine_barrier()

    es4 = ExitStack()
    w3f = es4.enter_context(nc.sbuf_tensor("f3_wf", [128, 8 * 512], F32))
    sa3 = es4.enter_context(nc.sbuf_tensor("f3_sa", [128, 8 * 512], BF16))
    sb3 = es4.enter_context(nc.sbuf_tensor("f3_sb", [128, 8 * 512], BF16))
    t23 = es4.enter_context(nc.sbuf_tensor("f3_t2", [128, 2, (FF // 128) * 512], BF16))
    k3f = es4.enter_context(nc.sbuf_tensor("f3_k3", [128, 2, FF], BF16))
    alpha3 = es4.enter_context(nc.sbuf_tensor("f3_al3", [128, ntc], F32))
    o_t = es4.enter_context(nc.sbuf_tensor("f3_o", [128, 2, 512], F32))
    ps3 = es4.enter_context(nc.psum_tensor("f3_ps", [128, 2, 512], F32))

    vf(c, nc.vector.reciprocal(alpha3[:], c.r_sc3[:]))
    vf(c, nc.vector.tensor_scalar(alpha3[:], alpha3[:], c.acol3[:], None, ALU.mult))
    vf(c, nc.vector.tensor_scalar_mul(alpha3[:], alpha3[:], 0.5))

    k3_free = [None, None]
    o_free = [None, None]
    ps_free = [None, None]
    n_fc3 = FF // 128

    t23_free = [None, None]
    t23_done = [None, None]

    def mm3_tern(os_, tslot, qq):
        nc.sync.wait_ge(c.s_act.h, c.s_act.v)
        w_ld = dma_group(c, nc.sync, c.d["wt0"],
                         [(w3f[:, i * 512:(i + 1) * 512],
                           c.w3T[(qq * 8 + i) * 128:(qq * 8 + i + 1) * 128,
                                 os_ * 512:(os_ + 1) * 512])
                          for i in range(8)])
        nc.scalar.wait_ge(c.d["wt0"].h, w_ld)
        if qq == 0 and t23_free[tslot] is not None:
            nc.scalar.wait_ge(c.s_pe.h, t23_free[tslot])
            nc.vector.wait_ge(c.s_pe.h, t23_free[tslot])
        ternarize(c, w3f[:], 8 * 512, sa3, sb3,
                  t23[:, tslot, qq * 8 * 512:(qq + 1) * 8 * 512],
                  c.thr3, c.nthr3)
        return c.s_dve.v

    for qq in range(8):
        t23_done[0] = mm3_tern(0, 0, qq)
    k3_ld = [None, None]

    def mm3_kload(tc):
        si = tc % 2
        if k3_free[si] is not None:
            nc.sync.wait_ge(c.s_pe.h, k3_free[si])
        k3_ld[si] = dma(c, nc.sync, ld_sems[si], k3f[:, si], c.k3fm[tc])

    for os_ in range(DM // 512):
        tslot = os_ % 2
        mm3_kload(0)
        for tc in range(ntc):
            si = tc % 2
            if tc + 1 < ntc:
                mm3_kload(tc + 1)
            if os_ + 1 < DM // 512:
                per = -(-8 // ntc)
                if tc == 0:
                    t23_free[(os_ + 1) % 2] = c.s_pe.v
                for qq in range(tc * per, min((tc + 1) * per, 8)):
                    t23_done[(os_ + 1) % 2] = mm3_tern(os_ + 1, (os_ + 1) % 2, qq)
            pj = tc % 2
            if ps_free[pj] is not None:
                nc.tensor.wait_ge(c.s_dve.h, ps_free[pj])
            nc.tensor.wait_ge(ld_sems[si].h, k3_ld[si])
            if tc == 0:
                nc.tensor.wait_ge(c.s_dve.h, t23_done[tslot])
            inst = None
            for fc in range(n_fc3):
                inst = nc.tensor.matmul(
                    ps3[:, pj], k3f[:, si, fc * 128:(fc + 1) * 128],
                    t23[:, tslot, fc * 512:(fc + 1) * 512],
                    start=(fc == 0), stop=(fc == n_fc3 - 1))
            inst.then_inc(c.s_pe.h, 1)
            mm_done = c.s_pe.inc(1)
            k3_free[si] = mm_done
            oj = tc % 2
            if o_free[oj] is not None:
                nc.vector.wait_ge(c.d["st2"].h, o_free[oj])
            nc.vector.wait_ge(c.s_pe.h, mm_done)
            nc.vector.tensor_scalar(o_t[:, oj], ps3[:, pj],
                                    alpha3[:, tc:tc + 1], None,
                                    ALU.mult).then_inc(c.s_dve.h, 1)
            ev_done = c.s_dve.inc(1)
            ps_free[pj] = ev_done
            nc.sync.wait_ge(c.s_dve.h, ev_done)
            o_free[oj] = dma(c, nc.sync, c.d["st2"],
                             c.ffn_s[tc][:, os_ * 512:(os_ + 1) * 512], o_t[:, oj])
    nc.sync.wait_ge(c.d["st2"].h, c.d["st2"].v)
    es4.close()
    es0.close()


def phase_qside(c):
    nc = c.nc
    ntc = c.ntc
    es2 = ExitStack()
    wtmp = es2.enter_context(nc.sbuf_tensor("qs_wtmp", [128, 2, 4 * 512], F32))
    wq_h = es2.enter_context(nc.sbuf_tensor("qs_wqh", [128, NDC * 512], BF16))
    wq_l = es2.enter_context(nc.sbuf_tensor("qs_wql", [128, NDC * 512], BF16))
    wg_h = es2.enter_context(nc.sbuf_tensor("qs_wgh", [128, NDC * 512], BF16))
    wg_l = es2.enter_context(nc.sbuf_tensor("qs_wgl", [128, NDC * 512], BF16))
    xf_t = es2.enter_context(nc.sbuf_tensor("qs_xf", [128, 2, 2, DM], BF16))
    bqbc = es2.enter_context(nc.sbuf_tensor("qs_bq", [128, 512], F32))
    bgbc = es2.enter_context(nc.sbuf_tensor("qs_bg", [128, 512], F32))
    ext_h = es2.enter_context(nc.sbuf_tensor("qs_exth", [128, G * NFC * 129], BF16))
    ext_l = es2.enter_context(nc.sbuf_tensor("qs_extl", [128, G * NFC * 129], BF16))
    ccn_t = es2.enter_context(nc.sbuf_tensor("qs_ccn", [128, NFC * G], F32))
    q_tm = es2.enter_context(nc.sbuf_tensor("qs_q", [128, 2, 512], F32))
    x2e = es2.enter_context(nc.sbuf_tensor("qs_x2e", [128, 2, G * 259], F32))
    u_tm = es2.enter_context(nc.sbuf_tensor("qs_u", [128, G * FD], F32))
    u_h = es2.enter_context(nc.sbuf_tensor("qs_uh", [128, 2, G * FD], BF16))
    u_l = es2.enter_context(nc.sbuf_tensor("qs_ul", [128, 2, G * FD], BF16))
    ufm_h = es2.enter_context(nc.sbuf_tensor("qs_ufh", [128, G * FD], BF16))
    ufm_l = es2.enter_context(nc.sbuf_tensor("qs_ufl", [128, G * FD], BF16))
    sqs = es2.enter_context(nc.sbuf_tensor("qs_sqs", [128, FD], F32))
    mh_t = es2.enter_context(nc.sbuf_tensor("qs_mh", [128, 2, 512], F32))
    gate_t = es2.enter_context(nc.sbuf_tensor("qs_gate", [128, 2, 512], F32))
    hsx_t = es2.enter_context(nc.sbuf_tensor("qs_hsx", [128, 2, 512], F32))
    ffn_t = es2.enter_context(nc.sbuf_tensor("qs_ffn", [128, 2, 512], F32))
    o_t = es2.enter_context(nc.sbuf_tensor("qs_o", [128, 2, 512], F32))
    kr = es2.enter_context(nc.sbuf_tensor("qs_kr", [128, 2, 16], F32))
    onep = es2.enter_context(nc.sbuf_tensor("qs_onep", [128, ntc], F32))
    extf = es2.enter_context(nc.sbuf_tensor("qs_extf", [128, 2, 129], F32))
    mstage = es2.enter_context(nc.sbuf_tensor("qs_mst", [128, 2, 128], F32))
    ps_q = es2.enter_context(nc.psum_tensor("qs_psq", [128, 2, 512], F32))
    ps_g = es2.enter_context(nc.psum_tensor("qs_psg", [128, 2, 512], F32))
    ps_na = es2.enter_context(nc.psum_tensor("qs_psna", [128, 2, 512], F32))
    ps_tr = es2.enter_context(nc.psum_tensor("qs_pstr", [128, 2, 1024], BF16))

    if c.use_cc:
        nc.sync.wait_ge(c.s_ccs.h, c.s_ccs.v)
        nc.gpsimd.wait_ge(c.s_ccs.h, c.s_ccs.v)
        nc.vector.wait_ge(c.s_ccs.h, c.s_ccs.v)
    vf(c, nc.vector.tensor_scalar_add(onep[:], c.r_rstd[:], 1.0))

    ld_sems = [c.d["ld0"], c.d["ld1"]]
    hl_sems = [c.d["ld2"], c.d["ld3"]]
    ca_sems = [c.d["ca2"], c.d["ca3"]]
    st_sems = [c.d["st0"], c.d["st1"]]
    tr_free = [None, None]
    na_free = [None, None]
    lane = [dict(), dict()]

    for os_ in range(G):
        heads = range(4 * os_, 4 * os_ + 4)
        nc.sync.wait_ge(c.s_pe.h, c.s_pe.v)
        for (WT, wh, wl) in ((c.WqT, wq_h, wq_l), (c.WgT, wg_h, wg_l)):
            load_split_pipelined(c, WT, os_ * 512, 512, wh, wl, wtmp,
                                 c.d["wt0"], c.d["ca0"])
        w_split = c.s_dve.v
        b_rq = dma_bcast_row(c, nc.gpsimd, bqbc[:],
                             c.bq[0:1, os_ * 512:(os_ + 1) * 512], c.d["bc0"])
        b_rg = dma_bcast_row(c, nc.gpsimd, bgbc[:],
                             c.bg[0:1, os_ * 512:(os_ + 1) * 512], c.d["bc0"])
        nc.sync.wait_ge(c.s_dve.h, c.s_dve.v)
        ccn_ld = dma(c, nc.sync, c.d["wt1"], ccn_t[:], c.cc_n[os_])
        eslots = [c.d["wt1"], c.d["dbg"]]
        eslot_free = [None, None]
        for ei, (ji, fc) in enumerate([(j, f) for j in range(G)
                                       for f in range(NFC)]):
            h = 4 * os_ + ji
            es_ = ei % 2
            t0 = ((ji * NFC) + fc) * 129
            if eslot_free[es_] is not None:
                nc.sync.wait_ge(c.s_dve.h, eslot_free[es_])
            eld = dma_group(c, nc.sync, eslots[es_],
                            [(extf[:, es_, 0:128],
                              c.mem[h, fc * 128:(fc + 1) * 128, :]),
                             (extf[:, es_, 128:129],
                              c.nrm[h, fc * 128:(fc + 1) * 128]
                              .rearrange("(f o) -> f o", o=1)),
                             (mstage[:, es_],
                              c.cc_m[os_, fc][:, ji * 128:(ji + 1) * 128])])
            nc.vector.wait_ge(eslots[es_].h, eld)
            if ei == 0:
                nc.vector.wait_ge(c.d["wt1"].h, ccn_ld)
            vf(c, nc.vector.tensor_add(extf[:, es_, 0:128], extf[:, es_, 0:128],
                                       mstage[:, es_]))
            vf(c, nc.vector.tensor_add(extf[:, es_, 128:129],
                                       extf[:, es_, 128:129],
                                       ccn_t[:, fc * 4 + ji:fc * 4 + ji + 1]))
            vf(c, nc.vector.tensor_copy(ext_h[:, t0:t0 + 129], extf[:, es_]))
            nc.vector.tensor_sub(ext_l[:, t0:t0 + 129], extf[:, es_],
                                 ext_h[:, t0:t0 + 129]).then_inc(c.s_dve.h, 1)
            eslot_free[es_] = c.s_dve.inc(1)
        ext_done = c.s_dve.v

        def stage_load(tc):
            L = tc % 2
            lk = lane[L]
            if "mm" in lk:
                nc.sync.wait_ge(c.s_pe.h, lk["mm"])
            lk["xld"] = dma_group(c, nc.sync, ld_sems[L],
                                  [(xf_t[:, L, 0], c.xnfm_h[tc]),
                                   (xf_t[:, L, 1], c.xnfm_l[tc])])

        def stage_a(tc):
            L = tc % 2
            lk = lane[L]
            nc.tensor.wait_ge(ld_sems[L].h, lk["xld"])
            if tc <= 1:
                nc.tensor.wait_ge(c.s_dve.h, w_split)
            if "qe" in lk:
                nc.tensor.wait_ge(c.s_dve.h, lk["qe"])
            if "ge_add" in lk:
                nc.tensor.wait_ge(c.s_dve.h, lk["ge_add"])
            inst = None
            for wh, wl, pst in ((wq_h, wq_l, ps_q), (wg_h, wg_l, ps_g)):
                n = 0
                for dc in range(NDC):
                    for lhs, rhs in ((0, wh), (1, wh), (0, wl)):
                        inst = nc.tensor.matmul(
                            pst[:, L], xf_t[:, L, lhs, dc * 128:(dc + 1) * 128],
                            rhs[:, dc * 512:(dc + 1) * 512],
                            start=(n == 0), stop=(n == NDC * 3 - 1))
                        n += 1
            inst.then_inc(c.s_pe.h, 1)
            lk["mm"] = c.s_pe.inc(1)
            nc.vector.wait_ge(c.s_pe.h, lk["mm"])
            if tc <= 1:
                nc.vector.wait_ge(c.d["bc0"].h, b_rq)
            if "x2" in lk:
                nc.vector.wait_ge(c.s_act.h, lk["x2"])
            nc.vector.tensor_add(q_tm[:, L], ps_q[:, L],
                                 bqbc[:]).then_inc(c.s_dve.h, 1)
            lk["qe"] = c.s_dve.inc(1)
            nc.scalar.wait_ge(c.s_dve.h, lk["qe"])
            if "u" in lk:
                nc.scalar.wait_ge(c.s_dve.h, lk["u"])
            inst = None
            for j in range(G):
                b0 = j * 259
                inst = nc.scalar.activation(x2e[:, L, b0 + 3:b0 + 131],
                                            q_tm[:, L, j * 128:(j + 1) * 128],
                                            AF.Relu)
                inst = nc.scalar.activation(x2e[:, L, b0 + 131:b0 + 259],
                                            q_tm[:, L, j * 128:(j + 1) * 128],
                                            AF.Relu, scale=-1.0)
            inst.then_inc(c.s_act.h, 1)
            lk["x2"] = c.s_act.inc(1)
            nc.vector.wait_ge(c.s_act.h, lk["x2"])
            inst = None
            for j in range(G):
                b0 = j * 259
                inst = nc.vector.tensor_copy(x2e[:, L, b0:b0 + 3],
                                             x2e[:, L, b0 + 256:b0 + 259])
            inst.then_inc(c.s_dve.h, 1)
            nc.vector.wait_ge(c.s_dve.h, c.s_dve.inc(1))
            other = lane[1 - L]
            if "sq" in other:
                nc.vector.wait_ge(c.s_act.h, other["sq"])
            if "ulo" in other:
                nc.vector.wait_ge(c.s_dve.h, other["ulo"])
            if "uhi" in other:
                nc.vector.wait_ge(ca_sems[1 - L].h, other["uhi"])
            inst = None
            for j in range(G):
                b0 = j * 259
                for jb in range(NU):
                    o0 = j * FD + jb * 256
                    inst = nc.vector.tensor_mul(
                        u_tm[:, o0:o0 + 256],
                        x2e[:, L, b0 + 3:b0 + 259],
                        x2e[:, L, b0 + 2 - jb:b0 + 258 - jb])
            inst.then_inc(c.s_dve.h, 1)
            lk["u"] = c.s_dve.inc(1)
            nc.scalar.wait_ge(c.s_dve.h, lk["u"])
            if "rows" in lk:
                nc.scalar.wait_ge(c.s_dve.h, lk["rows"])
            for j in range(G):
                af(c, nc.scalar.activation(sqs[:],
                                           u_tm[:, j * FD:(j + 1) * FD],
                                           AF.Square, accum_out=kr[:, L, j:j + 1]))
            nc.vector.wait_ge(c.s_act.h, c.s_act.v)
            vf(c, nc.vector.tensor_scalar_max(kr[:, L, 4:8], kr[:, L, 0:4],
                                               1e-30))
            nc.scalar.wait_ge(c.s_dve.h, c.s_dve.v)
            af(c, nc.scalar.activation(kr[:, L, 4:8], kr[:, L, 4:8], AF.Ln))
            nc.scalar.activation(kr[:, L, 4:8], kr[:, L, 4:8], AF.Exp,
                                 scale=0.5).then_inc(c.s_act.h, 1)
            lk["sq"] = c.s_act.inc(1)
            nc.vector.wait_ge(c.s_act.h, lk["sq"])
            vf(c, nc.vector.tensor_scalar_max(kr[:, L, 4:8], kr[:, L, 4:8], L2_EPS))
            nc.vector.tensor_scalar_mul(kr[:, L, 8:12], kr[:, L, 4:8],
                                        EPS).then_inc(c.s_dve.h, 1)
            lk["rows"] = c.s_dve.inc(1)
            nc.gpsimd.wait_ge(c.s_dve.h, lk["u"])
            if "tr" in lk:
                nc.gpsimd.wait_ge(c.s_act.h, lk["tr"])
            lk["uhi"] = dma(c, nc.gpsimd, ca_sems[L], u_h[:, L], u_tm[:])
            nc.vector.wait_ge(ca_sems[L].h, lk["uhi"])
            nc.vector.tensor_sub(u_l[:, L], u_tm[:],
                                 u_h[:, L]).then_inc(c.s_dve.h, 1)
            lk["ulo"] = c.s_dve.inc(1)

        def stage_b(tc):
            L = tc % 2
            lk = lane[L]
            other = lane[1 - L]
            if "na" in other:
                nc.scalar.wait_ge(c.s_pe.h, other["na"])
            if "vm" in other:
                nc.tensor.wait_ge(c.s_dve.h, other["vm"])
            nc.tensor.wait_ge(ca_sems[L].h, lk["uhi"])
            transpose_tiles(c, u_h[:, L], G * NFC, ufm_h[:], ps_tr, tr_free)
            nc.tensor.wait_ge(c.s_dve.h, lk["ulo"])
            transpose_tiles(c, u_l[:, L], G * NFC, ufm_l[:], ps_tr, tr_free)
            lk["tr"] = c.s_act.v
            for ji in range(G):
                nb = ji % 2
                nc.tensor.wait_ge(c.s_act.h, lk["tr"])
                if tc <= 1 and ji == 0:
                    nc.tensor.wait_ge(c.s_dve.h, ext_done)
                if na_free[nb] is not None:
                    nc.tensor.wait_ge(c.s_dve.h, na_free[nb])
                n = 0
                inst = None
                for fc in range(NFC):
                    s0 = (ji * NFC + fc) * 128
                    t0 = (ji * NFC + fc) * 129
                    for lhs, rhs in ((ufm_h, ext_h), (ufm_h, ext_l), (ufm_l, ext_h)):
                        inst = nc.tensor.matmul(
                            ps_na[:, nb, 0:129], lhs[:, s0:s0 + 128],
                            rhs[:, t0:t0 + 129],
                            start=(n == 0), stop=(n == NFC * 3 - 1))
                        n += 1
                inst.then_inc(c.s_pe.h, 1)
                na_done = c.s_pe.inc(1)
                nc.vector.wait_ge(c.s_pe.h, na_done)
                vf(c, nc.vector.tensor_scalar(kr[:, L, 12 + nb:13 + nb],
                                              ps_na[:, nb, 128:129],
                                              kr[:, L, 8 + ji:9 + ji],
                                              None, ALU.add))
                vf(c, nc.vector.reciprocal(kr[:, L, 12 + nb:13 + nb],
                                           kr[:, L, 12 + nb:13 + nb]))
                vf(c, nc.vector.tensor_scalar(mh_t[:, L, ji * 128:(ji + 1) * 128],
                                              ps_na[:, nb, 0:128],
                                              kr[:, L, 12 + nb:13 + nb],
                                              None, ALU.mult))
                na_free[nb] = c.s_dve.v
                lk["vm"] = c.s_dve.v
                lk["na"] = na_done
            nc.vector.wait_ge(c.s_pe.h, lk["mm"])
            if tc <= 1:
                nc.vector.wait_ge(c.d["bc0"].h, b_rg)
            if "gsig" in lk:
                nc.vector.wait_ge(c.s_dve.h, lk["gsig"])
            nc.vector.tensor_add(gate_t[:, L], ps_g[:, L],
                                 bgbc[:]).then_inc(c.s_dve.h, 1)
            lk["ge_add"] = c.s_dve.inc(1)
            nc.scalar.wait_ge(c.s_dve.h, lk["ge_add"])
            nc.scalar.activation(gate_t[:, L], gate_t[:, L], AF.Exp,
                                 scale=-1.0).then_inc(c.s_act.h, 1)
            ge_exp = c.s_act.inc(1)
            nc.vector.wait_ge(c.s_act.h, ge_exp)
            vf(c, nc.vector.tensor_scalar_add(gate_t[:, L], gate_t[:, L], 1.0))
            nc.vector.reciprocal(gate_t[:, L],
                                 gate_t[:, L]).then_inc(c.s_dve.h, 1)
            lk["gsig"] = c.s_dve.inc(1)
            if "hsx" in lk:
                nc.sync.wait_ge(c.s_act.h, lk["hsx"])
            if "cmb" in lk:
                nc.sync.wait_ge(c.s_dve.h, lk["cmb"])
            hld = dma(c, nc.sync, hl_sems[L], ffn_t[:, L],
                      c.hs[tc * 128:(tc + 1) * 128, os_ * 512:(os_ + 1) * 512])
            nc.scalar.wait_ge(hl_sems[L].h, hld)
            if "cmb" in lk:
                nc.scalar.wait_ge(c.s_dve.h, lk["cmb"])
            nc.scalar.activation(hsx_t[:, L], ffn_t[:, L], AF.Identity,
                                 bias=c.r_nrstd[:, tc:tc + 1],
                                 scale=onep[:, tc:tc + 1]).then_inc(c.s_act.h, 1)
            lk["hsx"] = c.s_act.inc(1)
            nc.sync.wait_ge(c.s_act.h, lk["hsx"])
            fld = dma(c, nc.sync, hl_sems[L], ffn_t[:, L],
                      c.ffn_s[tc][:, os_ * 512:(os_ + 1) * 512])
            if "st" in lk:
                nc.vector.wait_ge(st_sems[L].h, lk["st"])
            nc.vector.wait_ge(c.s_dve.h, lk["gsig"])
            nc.vector.wait_ge(c.s_act.h, lk["hsx"])
            nc.vector.wait_ge(hl_sems[L].h, fld)
            vf(c, nc.vector.tensor_mul(o_t[:, L], gate_t[:, L], mh_t[:, L]))
            vf(c, nc.vector.tensor_add(o_t[:, L], o_t[:, L], ffn_t[:, L]))
            nc.vector.tensor_add(o_t[:, L], o_t[:, L],
                                 hsx_t[:, L]).then_inc(c.s_dve.h, 1)
            lk["cmb"] = c.s_dve.inc(1)
            nc.sync.wait_ge(c.s_dve.h, lk["cmb"])
            lk["st"] = dma(c, nc.sync, st_sems[L],
                           c.out[tc * 128:(tc + 1) * 128,
                                 os_ * 512:(os_ + 1) * 512], o_t[:, L])

        for i in range(ntc + 2):
            if i < ntc:
                stage_load(i)
            if 1 <= i <= ntc:
                stage_a(i - 1)
            if i >= 2:
                stage_b(i - 2)

    for s in ("st0", "st1"):
        nc.sync.wait_ge(c.d[s].h, c.d[s].v)
    es2.close()


def kernel(**inp):
    hs = np.ascontiguousarray(np.asarray(inp["L0_attn_hidden_states"])[0])
    S = hs.shape[0]
    percore = S // NCORES
    ntc = percore // 128

    common = {
        "WqT": np.ascontiguousarray(np.asarray(inp["L0_attn_Wq"]).T),
        "WkT": np.ascontiguousarray(np.asarray(inp["L0_attn_Wk"]).T),
        "WvT": np.ascontiguousarray(np.asarray(inp["L0_attn_Wv"]).T),
        "WgT": np.ascontiguousarray(np.asarray(inp["L0_attn_Wg"]).T),
        "WmbT": np.ascontiguousarray(np.asarray(inp["L0_attn_Wmb"]).T),
        "bq": np.asarray(inp["L0_attn_bq"]).reshape(1, -1),
        "bk": np.asarray(inp["L0_attn_bk"]).reshape(1, -1),
        "bv": np.asarray(inp["L0_attn_bv"]).reshape(1, -1),
        "bg": np.asarray(inp["L0_attn_bg"]).reshape(1, -1),
        "bmb": np.asarray(inp["L0_attn_bmb"]).reshape(1, -1),
        "mem": np.ascontiguousarray(np.asarray(inp["L0_attn_memory"])),
        "nrm": np.ascontiguousarray(np.asarray(inp["L0_attn_norm"])),
        "w1T": np.ascontiguousarray(np.asarray(inp["L0_ffn_w1"]).T),
        "w2T": np.ascontiguousarray(np.asarray(inp["L0_ffn_w2"]).T),
        "w3T": np.ascontiguousarray(np.asarray(inp["L0_ffn_w3"]).T),
        "g1": np.asarray(inp["L0_ffn_g1"]).reshape(1, -1),
        "g2": np.asarray(inp["L0_ffn_g2"]).reshape(1, -1),
        "g3": np.asarray(inp["L0_ffn_g3"]).reshape(1, -1),
    }
    nc = build_program(ntc)
    in_maps = [dict(common, hs=np.ascontiguousarray(hs[i * percore:(i + 1) * percore]))
               for i in range(NCORES)]
    res = run_bass_kernel_spmd(nc, in_maps, core_ids=list(range(NCORES)))
    out = np.concatenate([np.asarray(r["out"]) for r in res.results], axis=0)
    return out.reshape(1, S, DM)
def phase_p1(c):
    nc = c.nc
    ntc = c.ntc
    es2 = ExitStack()
    hs_t = es2.enter_context(nc.sbuf_tensor("p1_hs", [128, 2, DM], F32))
    xn_t = es2.enter_context(nc.sbuf_tensor("p1_xn", [128, DM], F32))
    xh_t = es2.enter_context(nc.sbuf_tensor("p1_xh", [128, DM], BF16))
    xl_t = es2.enter_context(nc.sbuf_tensor("p1_xl", [128, DM], BF16))
    sq_t = es2.enter_context(nc.sbuf_tensor("p1_sq", [128, DM], F32))
    m_t = es2.enter_context(nc.sbuf_tensor("p1_m", [128, DM], F32))
    phi_t = es2.enter_context(nc.sbuf_tensor("p1_phi", [128, DM], F32))
    ci_t = es2.enter_context(nc.sbuf_tensor("p1_ci", [128, DM], I16))
    cf_t = es2.enter_context(nc.sbuf_tensor("p1_cf", [128, DM], F32))
    k_t = es2.enter_context(nc.sbuf_tensor("p1_k", [128, DM], BF16))
    stage = es2.enter_context(nc.sbuf_tensor("p1_stage", [128, 2, DM], BF16))
    g1bc = es2.enter_context(nc.sbuf_tensor("p1_g1bc", [128, DM], F32))
    g2bc = es2.enter_context(nc.sbuf_tensor("p1_g2bc", [128, DM], F32))
    rows = es2.enter_context(nc.sbuf_tensor("p1_rows", [128, 8], F32))
    ps_tr = es2.enter_context(nc.psum_tensor("p1_pstr", [128, 2, 1024], BF16))

    dma_bcast_row(c, nc.gpsimd, g1bc[:], c.g1[0:1, :], c.d["bc0"])
    g_ready = dma_bcast_row(c, nc.gpsimd, g2bc[:], c.g2[0:1, :], c.d["bc0"])

    ld_sems = [c.d["ld0"], c.d["ld1"]]
    st_sems = [c.d["st0"], c.d["st1"]]
    hs_free = [None, None]
    tr_free = [None, None]

    for tc in range(ntc):
        nw = len(c.wmean_pieces)
        per = -(-nw // ntc) if nw else 0
        for i in range(tc * per, min((tc + 1) * per, nw)):
            c.wmean_pieces[i]()
        si = tc % 2
        if hs_free[si] is not None:
            nc.sync.wait_ge(c.s_act.h, hs_free[si])
        hs_ld = dma(c, nc.sync, ld_sems[si], hs_t[:, si],
                    c.hs[tc * 128:(tc + 1) * 128, :])

        nc.vector.wait_ge(ld_sems[si].h, hs_ld)
        vf(c, nc.vector.reduce_sum(rows[:, 0:1], hs_t[:, si], AX.X))
        vf(c, nc.vector.tensor_scalar_mul(c.r_mean[:, tc:tc + 1], rows[:, 0:1],
                                          1.0 / DM))
        nc.scalar.wait_ge(ld_sems[si].h, hs_ld)
        nc.scalar.wait_ge(c.s_dve.h, c.s_dve.v)
        nc.scalar.activation(sq_t[:], hs_t[:, si], AF.Square,
                             accum_out=rows[:, 1:2]).then_inc(c.s_act.h, 1)
        sq_done = c.s_act.inc(1)
        nc.vector.wait_ge(c.s_act.h, sq_done)
        vf(c, nc.vector.tensor_scalar_mul(rows[:, 1:2], rows[:, 1:2], 1.0 / DM))
        vf(c, nc.vector.tensor_mul(rows[:, 2:3], c.r_mean[:, tc:tc + 1],
                                   c.r_mean[:, tc:tc + 1]))
        vf(c, nc.vector.tensor_sub(rows[:, 1:2], rows[:, 1:2], rows[:, 2:3]))
        nc.vector.tensor_scalar_add(rows[:, 2:3], rows[:, 1:2],
                                    LN_EPS).then_inc(c.s_dve.h, 1)
        var_done = c.s_dve.inc(1)
        nc.scalar.wait_ge(c.s_dve.h, var_done)
        nc.scalar.activation(rows[:, 3:4], rows[:, 2:3], AF.Sqrt).then_inc(c.s_act.h, 1)
        sqrt_done = c.s_act.inc(1)
        nc.vector.wait_ge(c.s_act.h, sqrt_done)
        vf(c, nc.vector.reciprocal(c.r_rstd[:, tc:tc + 1], rows[:, 3:4]))
        vf(c, nc.vector.tensor_mul(c.r_nrstd[:, tc:tc + 1],
                                   c.r_rstd[:, tc:tc + 1],
                                   c.r_mean[:, tc:tc + 1]))
        nc.vector.tensor_scalar_mul(c.r_nrstd[:, tc:tc + 1],
                                    c.r_nrstd[:, tc:tc + 1],
                                    -1.0).then_inc(c.s_dve.h, 1)
        rstd_done = c.s_dve.inc(1)
        nc.scalar.wait_ge(c.s_dve.h, rstd_done)
        nc.scalar.wait_ge(c.s_dve.h, c.s_dve.v)
        nc.scalar.wait_ge(c.d["ca0"].h, c.d["ca0"].v)
        if "xn_dbg" in c.debug_names:
            nc.scalar.wait_ge(c.d["dbg"].h, c.d["dbg"].v)
        nc.scalar.activation(xn_t[:], hs_t[:, si], AF.Identity,
                             bias=c.r_nrstd[:, tc:tc + 1],
                             scale=c.r_rstd[:, tc:tc + 1]).then_inc(c.s_act.h, 1)
        xn_done = c.s_act.inc(1)
        hs_free[si] = xn_done
        if "xn_dbg" in c.debug_names:
            nc.sync.wait_ge(c.s_act.h, xn_done)
            dma(c, nc.sync, c.d["dbg"], c.xn_dbg[tc], xn_t[:])

        nc.gpsimd.wait_ge(c.s_act.h, xn_done)
        xn_hi, xn_split = hilo_split(c, xn_t[:], xh_t[:], xl_t[:], c.d["ca0"])
        vf(c, nc.vector.tensor_mul(rows[:, 4:5], c.r_rstd[:, tc:tc + 1],
                                   c.r_rstd[:, tc:tc + 1]))
        vf(c, nc.vector.tensor_mul(rows[:, 4:5], rows[:, 4:5], rows[:, 1:2]))
        nc.vector.tensor_scalar_add(rows[:, 4:5], rows[:, 4:5],
                                    EPS).then_inc(c.s_dve.h, 1)
        msq_done = c.s_dve.inc(1)
        nc.scalar.wait_ge(c.s_dve.h, msq_done)
        nc.scalar.activation(rows[:, 5:6], rows[:, 4:5], AF.Sqrt).then_inc(c.s_act.h, 1)
        beta_sq = c.s_act.inc(1)
        nc.vector.wait_ge(c.s_act.h, beta_sq)
        vf(c, nc.vector.reciprocal(rows[:, 4:5], rows[:, 5:6]))

        for part, (srct, dst) in enumerate([(xh_t, c.xnfm_h), (xl_t, c.xnfm_l)]):
            pj = part
            nc.tensor.wait_ge(st_sems[pj].h, st_sems[pj].v)
            if part == 0:
                nc.tensor.wait_ge(c.d["ca0"].h, xn_hi)
            else:
                nc.tensor.wait_ge(c.s_dve.h, xn_split)
            ev = transpose_tiles(c, srct[:], NDC, stage[:, pj], ps_tr, tr_free)
            nc.sync.wait_ge(c.s_act.h, ev)
            dma(c, nc.sync, st_sems[pj], dst[tc], stage[:, pj])

        for which, (gbc, r_sc, kdst) in enumerate(
                [(g1bc, c.r_sc1, c.k1fm), (g2bc, c.r_sc2, c.k2fm)]):
            nc.vector.wait_ge(c.s_act.h, c.s_act.v)
            nc.vector.wait_ge(c.s_act.h, xn_done)
            if tc == 0 and which == 0:
                nc.vector.wait_ge(c.d["bc0"].h, g_ready)
            vf(c, nc.vector.tensor_mul(m_t[:], xn_t[:], gbc[:]))
            vf(c, nc.vector.reduce_max(rows[:, 6:7], m_t[:], AX.X,
                                       apply_absolute_value=True))
            vf(c, nc.vector.tensor_mul(rows[:, 6:7], rows[:, 6:7], rows[:, 4:5]))
            vf(c, nc.vector.tensor_scalar_max(rows[:, 6:7], rows[:, 6:7], Q_EPS))
            vf(c, nc.vector.reciprocal(rows[:, 7:8], rows[:, 6:7]))
            vf(c, nc.vector.tensor_scalar_mul(r_sc[:, tc:tc + 1], rows[:, 7:8],
                                              127.0))
            nc.vector.tensor_mul(rows[:, 6:7], r_sc[:, tc:tc + 1],
                                 rows[:, 4:5]).then_inc(c.s_dve.h, 1)
            gamma_done = c.s_dve.inc(1)
            nc.scalar.wait_ge(c.s_dve.h, gamma_done)
            nc.scalar.activation(phi_t[:], m_t[:], AF.Identity,
                                 bias=c.halfcol[:],
                                 scale=rows[:, 6:7]).then_inc(c.s_act.h, 1)
            phi_done = c.s_act.inc(1)
            nc.vector.wait_ge(c.s_act.h, phi_done)
            nc.vector.tensor_copy(ci_t[:], phi_t[:]).then_inc(c.s_dve.h, 1)
            ci_done = c.s_dve.inc(1)
            nc.scalar.wait_ge(c.s_dve.h, ci_done)
            nc.scalar.activation(cf_t[:], ci_t[:], AF.Copy).then_inc(c.s_act.h, 1)
            cf_done = c.s_act.inc(1)
            nc.vector.wait_ge(c.s_act.h, cf_done)
            vf(c, nc.vector.tensor_tensor(m_t[:], phi_t[:], cf_t[:], ALU.is_lt))
            nc.vector.wait_ge(c.s_pe.h, c.s_pe.v)
            nc.vector.tensor_sub(k_t[:], cf_t[:], m_t[:]).then_inc(c.s_dve.h, 1)
            k_done = c.s_dve.inc(1)
            pj = which
            nc.tensor.wait_ge(st_sems[pj].h, st_sems[pj].v)
            nc.tensor.wait_ge(c.s_dve.h, k_done)
            ev = transpose_tiles(c, k_t[:], NDC, stage[:, pj], ps_tr, tr_free)
            nc.sync.wait_ge(c.s_act.h, ev)
            dma(c, nc.sync, st_sems[pj], kdst[tc], stage[:, pj])

    if "rowdbg" in c.debug_names:
        nc.sync.wait_ge(c.s_dve.h, c.s_dve.v)
        dma_group(c, nc.sync, c.d["dbg"],
                  [(c.rowdbg[:, 0:ntc], c.r_mean[:]),
                   (c.rowdbg[:, 16:16 + ntc], c.r_rstd[:]),
                   (c.rowdbg[:, 32:32 + ntc], c.r_nrstd[:])])
    for s in ("st0", "st1", "dbg"):
        nc.sync.wait_ge(c.d[s].h, c.d[s].v)
    es2.close()


